# version 13
# baseline (speedup 1.0000x reference)
import numpy as np
from collections import defaultdict

import concourse.bass as bass
import concourse.mybir as mybir
from concourse.bass_utils import run_bass_kernel_spmd

F32 = mybir.dt.float32
BF16 = mybir.dt.bfloat16
I32 = mybir.dt.int32
I8 = mybir.dt.int8
AF = mybir.ActivationFunctionType
OP = mybir.AluOpType

DT_SIZE = {F32: 4, BF16: 2, I32: 4, I8: 1}


class Tracker:

    def __init__(self):
        self.spaces = defaultdict(list)

    @staticmethod
    def _merge(d, ch, val):
        if val > d.get(ch, -1):
            d[ch] = val

    def read(self, res, key):
        space, lo, hi = res
        waits = {}
        for ent in self.spaces[space]:
            if ent[0] < hi and lo < ent[1]:
                for c, v in ent[2].items():
                    self._merge(waits, c, v)
                if key is not None:
                    self._merge(ent[3], key[0], key[1])
        return waits

    def write(self, res, key):
        space, lo, hi = res
        waits = {}
        keep = []
        for ent in self.spaces[space]:
            if ent[0] < hi and lo < ent[1]:
                for c, v in ent[2].items():
                    self._merge(waits, c, v)
                for c, v in ent[3].items():
                    self._merge(waits, c, v)
                if ent[0] < lo:
                    keep.append([ent[0], lo, ent[2], ent[3]])
                if ent[1] > hi:
                    keep.append([hi, ent[1], dict(ent[2]), dict(ent[3])])
                continue
            keep.append(ent)
        keep.append([lo, hi, {key[0]: key[1]} if key else {}, {}])
        self.spaces[space] = keep
        return waits


class Ctx:
    def __init__(self, mode, eng=None, sems=None):
        self.mode = mode
        self.eng = eng
        self.sems = sems
        self.cnt = defaultdict(int)
        self.channels = set()
        self.trk = Tracker()
        self.credit = {m: defaultdict(int) for m in ("G", "S", "T", "A", "V")}
        self.nbar = 0
        self.marks = {}

    def _upd(self, ch, n=1):
        self.channels.add(ch)
        self.cnt[ch] += n
        return self.cnt[ch]

    def _collect(self, who, reads, writes, key):
        waits = {}
        for r in reads or ():
            for c, v in self.trk.read(r, key).items():
                Tracker._merge(waits, c, v)
        for w in writes or ():
            for c, v in self.trk.write(w, key).items():
                Tracker._merge(waits, c, v)
        if key is not None and waits.get(key[0], -1) >= key[1]:
            waits[key[0]] = key[1] - 1
        for c in list(waits):
            if c[0] == "d" and self.cnt[c] > waits[c] and (
                    key is None or c != key[0]):
                waits[c] = self.cnt[c]
        cr = self.credit[who]
        out = [(c, v) for c, v in waits.items() if cr[c] < v]
        for c, v in out:
            cr[c] = v
        return out

    def _emit_waits(self, pairs):
        for c, v in pairs:
            self.eng.wait_ge(self.sems[c], v)

    def dma(self, queue, stream, reads, writes, fn, extra=()):
        ch = "d" + stream
        val = self._upd(ch, 16)
        key = (ch, val)
        waits = self._collect(queue, reads, writes, key)
        for c, v in extra:
            if self.credit[queue][c] < v:
                self.credit[queue][c] = v
                waits.append((c, v))
        if self.mode == "pre" or self.mode != queue:
            return key
        self._emit_waits(waits)
        fn(self.eng).then_inc(self.sems[ch], 16)
        return key

    def op(self, eng, ch, reads, writes, fn, extra=(), as_key=None):
        key = None
        if ch is not None:
            key = (ch, self._upd(ch, 1))
        waits = self._collect(eng, reads, writes, as_key if as_key else key)
        for c, v in extra:
            if self.credit[eng][c] < v:
                self.credit[eng][c] = v
                waits.append((c, v))
        if self.mode == "pre" or self.mode != eng:
            return key
        self._emit_waits(waits)
        ins = fn(self.eng)
        if ch is not None:
            ins.then_inc(self.sems[ch], 1)
        return key

    def barrier(self):
        self.nbar += 1
        ch = f"bar{self.nbar}"
        self.channels.add(ch)
        if self.mode == "pre":
            return
        self.eng.nop().then_inc(self.sems[ch], 1)
        self.eng.wait_ge(self.sems[ch], 5)

    def mark(self, name):
        self.marks[name] = {c: self.cnt[c] for c in self.cnt if c.startswith("d")}

    def wait_mark(self, who, name, chans):
        pairs = []
        for c in chans:
            v = self.marks[name].get(c, 0)
            if v and self.credit[who][c] < v:
                self.credit[who][c] = v
                pairs.append((c, v))
        if self.mode == who:
            self._emit_waits(pairs)


def carve1f(arf_ap, off, n):
    return arf_ap[0:1, off:off + n]


def build_nc(D, F, TOK, SPW=1024, debug=False):
    DC = D // 128
    FC = F // 128
    NB = TOK // 2
    assert NB <= 512 and TOK % 2 == 0
    SPW = min(SPW, D)
    NSP = D // SPW
    SCP = SPW // 128
    TPW = min(512, D)
    NTP = D // TPW
    MB = min(4, SCP)
    NBUN = SCP // MB
    assert SCP % MB == 0 and DC % 4 == 0 and FC % 2 == 0 and MB == 4
    NSW = DC // 4

    nc = bass.Bass("TRN2", target_bir_lowering=False)

    xfm = nc.dram_tensor("xfm", [D, TOK], F32, kind="ExternalInput").ap()
    w1 = nc.dram_tensor("w1", [D, D], F32, kind="ExternalInput").ap()
    w2t = nc.dram_tensor("w2t", [D, D], F32, kind="ExternalInput").ap()
    maskd = nc.dram_tensor("maskd", [D, D], F32, kind="ExternalInput").ap()
    gwt = nc.dram_tensor("gwt", [D, F], I32, kind="ExternalInput").ap()
    uwt = nc.dram_tensor("uwt", [D, F], I32, kind="ExternalInput").ap()
    dwt = nc.dram_tensor("dwt", [F, D], I32, kind="ExternalInput").ap()
    gsd = nc.dram_tensor("gsd", [128, FC], F32, kind="ExternalInput").ap()
    usd = nc.dram_tensor("usd", [128, FC], F32, kind="ExternalInput").ap()
    dsd = nc.dram_tensor("dsd", [128, DC], F32, kind="ExternalInput").ap()
    b2d = nc.dram_tensor("b2d", [128, DC], F32, kind="ExternalInput").ap()
    outfm = nc.dram_tensor("outfm", [D, TOK], F32, kind="ExternalOutput").ap()

    w1bf_d = nc.dram_tensor("w1bf_d", [D, D], BF16, kind="Internal").ap()
    ikind = "ExternalOutput" if debug else "Internal"
    attn_d = nc.dram_tensor("attn_d", [D, D], BF16, kind=ikind).ap()
    res1_d = nc.dram_tensor("res1_d", [D, TOK], F32, kind=ikind).ap()
    h_d = nc.dram_tensor("h_d", [F, TOK], BF16, kind=ikind).ap()
    res2_d = nc.dram_tensor("res2_d", [D, TOK], F32, kind=ikind).ap()

    xfm_t = xfm.rearrange("(c p) w -> p c w", p=128)
    w1_t = w1.rearrange("(c p) w -> p c w", p=128)
    w2t_t = w2t.rearrange("(c p) w -> p c w", p=128)
    mask_t = maskd.rearrange("(c p) w -> p c w", p=128)
    gwt_t = gwt.rearrange("(c p) w -> p c w", p=128)
    uwt_t = uwt.rearrange("(c p) w -> p c w", p=128)
    dwt_t = dwt.rearrange("(c p) w -> p c w", p=128)
    attn_t = attn_d.rearrange("(c p) w -> p c w", p=128)
    w1bf_t = w1bf_d.rearrange("(c p) w -> p c w", p=128)
    res1_t = res1_d.rearrange("(c p) w -> p c w", p=128)
    h_t = h_d.rearrange("(c p) w -> p c w", p=128)
    res2_t = res2_d.rearrange("(c p) w -> p c w", p=128)

    ARB_EL = 36864
    ARF_EL = 13312
    import contextlib
    stack = contextlib.ExitStack()
    with stack:
        big0 = stack.enter_context(
            nc.sbuf_tensor("big0", [128, DC * max(SPW, TOK)], BF16))
        arb = stack.enter_context(nc.sbuf_tensor("arb", [128, ARB_EL], BF16))
        arf = stack.enter_context(nc.sbuf_tensor("arf", [128, ARF_EL], F32))
        acc_s = stack.enter_context(nc.sbuf_tensor("acc_s", [128, TOK], F32))
        acc_q = stack.enter_context(nc.sbuf_tensor("acc_q", [128, TOK], F32))
        bc_s = stack.enter_context(nc.sbuf_tensor("bc_s", [128, TOK], F32))
        bc_ms = stack.enter_context(nc.sbuf_tensor("bc_ms", [128, TOK], F32))
        b2t = stack.enter_context(nc.sbuf_tensor("b2t", [128, DC], F32))
        gst = stack.enter_context(nc.sbuf_tensor("gst", [128, FC], F32))
        ust = stack.enter_context(nc.sbuf_tensor("ust", [128, FC], F32))
        dst = stack.enter_context(nc.sbuf_tensor("dst", [128, DC], F32))
        onc = stack.enter_context(nc.sbuf_tensor("onc", [128, 1], F32))
        onr = stack.enter_context(nc.sbuf_tensor("onr", [1, 128], F32))
        ps = [stack.enter_context(nc.psum_tensor(f"ps{i}", [128, 512], F32))
              for i in range(8)]

        arb_ap, arf_ap = arb.ap(), arf.ap()
        big0v = big0.ap().rearrange("p (c w) -> p c w", c=DC)
        w1p = big0v[:, :, 0:SPW]
        xbf = big0v[:, :, 0:TOK]
        r1n = big0v[:, :, 0:TOK]
        psa = [p.ap() for p in ps]

        def cb(off, *shape):
            n = int(np.prod(shape))
            v = arb_ap[:, off:off + n]
            if len(shape) == 2:
                v = v.rearrange("p (a b) -> p a b", a=shape[0])
            return v

        def cf(off, *shape):
            n = int(np.prod(shape))
            v = arf_ap[:, off:off + n]
            if len(shape) == 2:
                v = v.rearrange("p (a b) -> p a b", a=shape[0])
            return v

        def sb(off, n):
            return ("ab", off, off + n)

        def sf(off, n):
            return ("af", off, off + n)

        TPE = DC * TPW
        w2tb = [cb(0, DC, TPW), cb(TPE, DC, TPW)]
        w2tb_r = [sb(0, TPE), sb(TPE, TPE)]
        attnb = [[cb(2 * TPE + (2 * j + pq) * TPW, TPW) for pq in range(2)]
                 for j in range(4)]
        attnb_r = [[sb(2 * TPE + (2 * j + pq) * TPW, TPW) for pq in range(2)]
                   for j in range(4)]
        assert 2 * TPE + 8 * TPW <= ARB_EL
        attnL = [cb(0, DC, 256), cb(DC * 256, DC, 256)]
        attnL_r = [sb(0, DC * 256), sb(DC * 256, DC * 256)]
        gwb = [cb(0, DC, 128), cb(DC * 128, DC, 128)]
        gwb_r = [sb(0, DC * 128), sb(DC * 128, DC * 128)]
        uwb = [cb(2 * DC * 128, DC, 128), cb(3 * DC * 128, DC, 128)]
        uwb_r = [sb(2 * DC * 128, DC * 128), sb(3 * DC * 128, DC * 128)]
        hbund = [cb(4 * DC * 128, TOK), cb(4 * DC * 128 + TOK, TOK)]
        hbund_r = [sb(4 * DC * 128, TOK), sb(4 * DC * 128 + TOK, TOK)]
        NDB = 3
        NBS = 3
        dwbB = [cb(k * NDB * 512, NDB, 512) for k in range(NBS)]
        dwbB_r = [sb(k * NDB * 512, NDB * 512) for k in range(NBS)]
        hb0 = NBS * NDB * 512
        hrdB = [cb(hb0 + k * NDB * TOK, NDB, TOK) for k in range(NBS)]
        hrdB_r = [sb(hb0 + k * NDB * TOK, NDB * TOK) for k in range(NBS)]
        assert hb0 + NBS * NDB * TOK <= ARB_EL

        maskb = [cf(0, MB, TPW), cf(MB * TPW, MB, TPW)]
        maskb_r = [sf(0, MB * TPW), sf(MB * TPW, MB * TPW)]
        x32b = [cf(0, TOK), cf(TOK, TOK)]
        x32b_r = [sf(0, TOK), sf(TOK, TOK)]
        yact = [cf(2 * TOK + k * NB, NB) for k in range(4)]
        yact_r = [sf(2 * TOK + k * NB, NB) for k in range(4)]
        sqtmp = cf(4 * TOK, NB)
        sqtmp_r = sf(4 * TOK, NB)
        rb0 = 4 * TOK + NB
        r1bund = [cf(rb0, TOK), cf(rb0 + TOK, TOK)]
        r1bund_r = [sf(rb0, TOK), sf(rb0 + TOK, TOK)]
        t1 = [cf(k * NB, NB) for k in range(4)]
        t1_r = [sf(k * NB, NB) for k in range(4)]
        r1rd = [cf(2 * TOK, TOK), cf(3 * TOK, TOK)]
        r1rd_r = [sf(2 * TOK, TOK), sf(3 * TOK, TOK)]
        tmp32 = cf(4 * TOK, TOK)
        tmp32_r = sf(4 * TOK, TOK)
        assert 5 * TOK <= rb0 + 0 * TOK or True
        r1rd2 = [cf(0, TOK), cf(TOK, TOK)]
        r1rd2_r = [sf(0, TOK), sf(TOK, TOK)]
        vtn = [cf(2 * TOK + k * NB, NB) for k in range(8)]
        vtn_r = [sf(2 * TOK + k * NB, NB) for k in range(8)]
        r2bund = [cf(6 * TOK, TOK), cf(7 * TOK, TOK)]
        r2bund_r = [sf(6 * TOK, TOK), sf(7 * TOK, TOK)]
        r1m = [cf(5 * TOK, TOK), cf(6 * TOK, TOK)]
        r1m_r = [sf(5 * TOK, TOK), sf(6 * TOK, TOK)]
        r2rd = [cf(0, TOK), cf(TOK, TOK)]
        r2rd_r = [sf(0, TOK), sf(TOK, TOK)]
        otmp = cf(2 * TOK, TOK)
        otmp_r = sf(2 * TOK, TOK)
        outb = r2rd
        outb_r = r2rd_r
        so = 8 * TOK
        st_s = carve1f(arf_ap, so, TOK)
        st_q = carve1f(arf_ap, so + TOK, TOK)
        tA = carve1f(arf_ap, so + 2 * TOK, TOK)
        tB = carve1f(arf_ap, so + 3 * TOK, TOK)
        rstd = carve1f(arf_ap, so + 4 * TOK, TOK)
        st_r = [sf(so + k * TOK, TOK) for k in range(5)]
        assert so + 5 * TOK <= ARF_EL

        B0 = ("b0", 0, DC)
        PS = lambda i: ("ps", i, i + 1)
        ACCS = lambda h: ("accs", h, h + 1)
        ACCQ = lambda h: ("accq", h, h + 1)
        BCS = ("bcs", 0, 1)
        BCMS = ("bcms", 0, 1)
        CONST = ("cst", 0, 1)
        ONES = ("ones", 0, 1)
        ATTN = ("attn", 0, 1)
        RES1 = lambda tc: ("res1", tc, tc + 1)
        HD = lambda fc: ("hd", fc, fc + 1)
        HD2 = lambda a, b: ("hd", a, b)
        RES2 = lambda tc: ("res2", tc, tc + 1)

        onc_ap, onr_ap = onc.ap(), onr.ap()
        accs, accq = acc_s.ap(), acc_q.ap()
        bcs, bcms = bc_s.ap(), bc_ms.ap()
        b2a, gsa, usa, dsa = b2t.ap(), gst.ap(), ust.ap(), dst.ap()

        def program(ctx):
            def pgroup(slot, ch, reads, mms, extra=(), mm_reads=None):
                n = len(mms)
                fk = (ch, ctx.cnt[ch] + 1)
                for i, m in enumerate(mms):
                    rd = list(reads) if i == 0 else []
                    if mm_reads and i in mm_reads:
                        rd += mm_reads[i]
                    if n == 1:
                        ctx.op("T", ch, rd, [PS(slot)], m, extra=extra)
                    elif i == 0:
                        ctx.op("T", None, rd, [PS(slot)], m, extra=extra,
                               as_key=fk)
                    elif i == n - 1:
                        ctx.op("T", ch, rd, [], m, as_key=fk)
                    else:
                        ctx.op("T", None, rd, [], m, as_key=fk)
                return fk

            def ln_finalize(ln):
                for h in (0, 1):
                    pgroup(h, f"mm{h}", [ACCS(h), ONES], [
                        lambda t, h=h: t.matmul(psa[h][0:1, 0:NB], onc_ap[:],
                                                accs[:, h * NB:(h + 1) * NB],
                                                start=True, stop=True)])
                    pgroup(2 + h, f"mm{2 + h}", [ACCQ(h), ONES], [
                        lambda t, h=h: t.matmul(psa[2 + h][0:1, 0:NB], onc_ap[:],
                                                accq[:, h * NB:(h + 1) * NB],
                                                start=True, stop=True)])
                for i, h in ((0, 0), (1, 1), (2, 0), (3, 1)):
                    dst_t, dr = (st_s, st_r[0]) if i < 2 else (st_q, st_r[1])
                    ctx.op("V", "lc", [PS(i)], [dr],
                           lambda v, i=i, h=h, dst_t=dst_t: v.tensor_copy(
                               dst_t[:, h * NB:(h + 1) * NB], psa[i][0:1, 0:NB]))
                inv = 1.0 / D
                ctx.op("V", "lnV", [st_r[0]], [st_r[0]],
                       lambda v: v.tensor_scalar_mul(st_s[:], st_s[:], inv))
                ctx.op("V", "lnV", [st_r[1]], [st_r[1]],
                       lambda v: v.tensor_scalar_mul(st_q[:], st_q[:], inv))
                ctx.op("V", "lnV", [st_r[0]], [st_r[2]],
                       lambda v: v.tensor_mul(tA[:], st_s[:], st_s[:]))
                ctx.op("V", "lnV", [st_r[1], st_r[2]], [st_r[3]],
                       lambda v: v.tensor_sub(tB[:], st_q[:], tA[:]))
                ctx.op("V", "lnV", [st_r[3]], [st_r[3]],
                       lambda v: v.tensor_scalar_add(tB[:], tB[:], 1e-5))
                ctx.op("V", "lnV", [st_r[3]], [st_r[2]],
                       lambda v: v.reciprocal(tA[:], tB[:]))
                ctx.op("A", "lnA", [st_r[2]], [st_r[4]],
                       lambda a: a.activation(rstd[:], tA[:], AF.Sqrt))
                ctx.op("V", "lnV", [st_r[4]], [st_r[2]],
                       lambda v: v.tensor_mul(tA[:], rstd[:], rstd[:]))
                ctx.op("V", "lnV", [st_r[2], st_r[3]], [st_r[2]],
                       lambda v: v.tensor_mul(tA[:], tA[:], tB[:]))
                ctx.op("V", "lnV", [st_r[2]], [st_r[2]],
                       lambda v: v.tensor_scalar(tA[:], tA[:], -0.5, 1.5,
                                                 OP.mult, OP.add))
                ctx.op("V", "lnV", [st_r[2], st_r[4]], [st_r[4]],
                       lambda v: v.tensor_mul(rstd[:], rstd[:], tA[:]))
                ctx.op("V", "lnV", [st_r[0], st_r[4]], [st_r[2]],
                       lambda v: v.tensor_mul(tA[:], st_s[:], rstd[:]))
                for h in (0, 1):
                    pgroup(4 + h, f"mm{4 + h}", [st_r[4], ONES], [
                        lambda t, h=h: t.matmul(psa[4 + h][:, 0:NB], onr_ap[:],
                                                rstd[:, h * NB:(h + 1) * NB],
                                                start=True, stop=True)])
                    ctx.op("V", "lc", [PS(4 + h)], [BCS],
                           lambda v, h=h: v.tensor_copy(
                               bcs[:, h * NB:(h + 1) * NB], psa[4 + h][:, 0:NB]))
                    pgroup(6 + h, f"mm{6 + h}", [st_r[2], ONES], [
                        lambda t, h=h: t.matmul(psa[6 + h][:, 0:NB], onr_ap[:],
                                                tA[:, h * NB:(h + 1) * NB],
                                                start=True, stop=True)])
                    ctx.op("V", "lc", [PS(6 + h)], [BCMS],
                           lambda v, h=h: v.tensor_copy(
                               bcms[:, h * NB:(h + 1) * NB], psa[6 + h][:, 0:NB]))

            ctx.dma("S", "cst", [], [CONST], lambda s: s.dma_start(out=b2a[:], in_=b2d))
            ctx.dma("S", "cst", [], [CONST], lambda s: s.dma_start(out=gsa[:], in_=gsd))
            ctx.dma("S", "cst", [], [CONST], lambda s: s.dma_start(out=usa[:], in_=usd))
            ctx.dma("S", "cst", [], [CONST], lambda s: s.dma_start(out=dsa[:], in_=dsd))
            ctx.op("V", "v0", [], [ONES], lambda v: v.memset(onc_ap[:], 1.0))
            ctx.op("V", "v0", [], [ONES], lambda v: v.memset(onr_ap[:], 1.0))
            ctx.op("V", "v0", [], [ACCS(0), ACCS(1)], lambda v: v.memzero(accs[:]))
            ctx.op("V", "v0", [], [ACCQ(0), ACCQ(1)], lambda v: v.memzero(accq[:]))

            SH = SCP // 2

            def emit_w1p(isp):
                for hp in range(2):
                    st = "gA" if hp == 0 else "gB"
                    dst = w1p[:, :, hp * SH * 128:(hp + 1) * SH * 128]
                    s0 = isp * SPW + hp * SH * 128
                    ctx.dma("G", st, [], [("b0s", hp, hp + 1)],
                            lambda g, s0=s0, dst=dst: g.dma_start(
                                out=dst, in_=w1_t[:, :, s0:s0 + SH * 128]))

            for isp in range(NSP):
                emit_w1p(isp)
                for itp in range(NTP):
                    gi = isp * NTP + itp
                    ctx.dma("G", f"g{gi % 2}", [], [w2tb_r[gi % 2]],
                            lambda g, itp=itp, gi=gi: g.dma_start(
                                out=w2tb[gi % 2],
                                in_=w2t_t[:, :, itp * TPW:(itp + 1) * TPW]))
                    for b in range(NBUN):
                        gmb = gi * NBUN + b
                        scg0 = isp * SCP + b * MB
                        ctx.dma("S", f"s{gmb % 2}", [], [maskb_r[gmb % 2]],
                                lambda s, scg0=scg0, itp=itp, gmb=gmb: s.dma_start(
                                    out=maskb[gmb % 2],
                                    in_=mask_t[:, scg0:scg0 + MB,
                                               itp * TPW:(itp + 1) * TPW]))
                        for j in range(MB):
                            isc = b * MB + j
                            shalf = isc // SH
                            mms = [
                                (lambda t, dc=dc, isc=isc, gi=gi, j=j:
                                 t.matmul(psa[j][:, 0:TPW],
                                          w1p[:, dc, isc * 128:(isc + 1) * 128],
                                          w2tb[gi % 2][:, dc, :],
                                          start=(dc == 0), stop=(dc == DC - 1)))
                                for dc in range(DC)]
                            pgroup(j, f"mm{j}",
                                   [("b0s", shalf, shalf + 1), w2tb_r[gi % 2]], mms)
                            ctx.op("V", f"ev{j}",
                                   [PS(j), maskb_r[gmb % 2]],
                                   [attnb_r[j][gmb % 2]],
                                   lambda v, j=j, gmb=gmb: v.tensor_tensor(
                                       attnb[j][gmb % 2][:], psa[j][:, 0:TPW],
                                       maskb[gmb % 2][:, j, :], OP.mult))
                            ctx.dma("A", f"ab{j}{gmb % 2}",
                                    [attnb_r[j][gmb % 2]],
                                    [("attn", gmb * MB + j, gmb * MB + j + 1)],
                                    lambda a, scg0=scg0, itp=itp, j=j, gmb=gmb:
                                    a.dma_start(
                                        out=attn_t[:, scg0 + j,
                                                   itp * TPW:(itp + 1) * TPW],
                                        in_=attnb[j][gmb % 2][:]))
            ctx.mark("p1s")


            XCH = ("gA", "gB", "gC", "gD")
            CKC = DC // 4
            for k in range(4):
                wr = [("b0", k * CKC, (k + 1) * CKC), ("b0s", 0, 2)]
                ctx.dma("G", XCH[k], [], wr,
                        lambda g, k=k: g.dma_start(
                            out=big0v[:, k * CKC:(k + 1) * CKC, 0:TOK],
                            in_=xfm_t[:, k * CKC:(k + 1) * CKC, :]))
            abch = [f"dab{j}{p}" for j in range(4) for p in range(2)]
            gk2 = 0
            for it2 in range(DC // 2):
                ctx.wait_mark("S", "p1s", abch)
                ctx.dma("S", f"s{it2 % 2}", [("attn", 0, 10 ** 6)],
                        [attnL_r[it2 % 2]],
                        lambda s, it2=it2: s.dma_start(
                            out=attnL[it2 % 2],
                            in_=attn_t[:, :, it2 * 256:(it2 + 1) * 256]))
                for tcl in range(2):
                    tc = 2 * it2 + tcl
                    ctx.dma("S", f"s{2 + tc % 2}", [], [x32b_r[tc % 2]],
                            lambda s, tc=tc: s.dma_start(
                                out=x32b[tc % 2], in_=xfm_t[:, tc, :]))
                    for h in range(2):
                        slot = gk2 % 8
                        mms = [
                            (lambda t, sc=sc, tcl=tcl, h=h, it2=it2, slot=slot:
                             t.matmul(psa[slot][:, 0:NB],
                                      attnL[it2 % 2][:, sc, tcl * 128:(tcl + 1) * 128],
                                      xbf[:, sc, h * NB:(h + 1) * NB],
                                      start=(sc == 0), stop=(sc == DC - 1)))
                            for sc in range(DC)]
                        mmrd = {k * CKC: [("b0", k * CKC, (k + 1) * CKC)]
                                for k in range(4)}
                        pgroup(slot, f"mm{slot}", [attnL_r[it2 % 2]], mms,
                               mm_reads=mmrd)
                        ctx.op("A", f"ga{gk2 % 4}",
                               [PS(slot), CONST], [yact_r[gk2 % 4]],
                               lambda a, gk2=gk2, slot=slot, tc=tc: a.activation(
                                   yact[gk2 % 4][:], psa[slot][:, 0:NB], AF.Gelu,
                                   bias=b2a[:, tc:tc + 1]))
                        rsl = r1bund[tc % 2][:, h * NB:(h + 1) * NB]
                        rres = ("af", r1bund_r[tc % 2][1] + h * NB,
                                r1bund_r[tc % 2][1] + (h + 1) * NB)
                        ctx.op("V", f"vr{gk2 % 4}",
                               [yact_r[gk2 % 4], x32b_r[tc % 2]], [rres],
                               lambda v, gk2=gk2, tc=tc, h=h, rsl=rsl: v.tensor_add(
                                   rsl, yact[gk2 % 4][:],
                                   x32b[tc % 2][:, h * NB:(h + 1) * NB]))
                        ctx.op("V", "lnr", [rres, ACCS(h)], [ACCS(h)],
                               lambda v, h=h, rsl=rsl: v.tensor_add(
                                   accs[:, h * NB:(h + 1) * NB],
                                   accs[:, h * NB:(h + 1) * NB], rsl))
                        ctx.op("V", "vq", [rres], [sqtmp_r],
                               lambda v, rsl=rsl: v.tensor_mul(sqtmp[:], rsl, rsl))
                        ctx.op("V", "lnr", [sqtmp_r, ACCQ(h)], [ACCQ(h)],
                               lambda v, h=h: v.tensor_add(
                                   accq[:, h * NB:(h + 1) * NB],
                                   accq[:, h * NB:(h + 1) * NB], sqtmp[:]))
                        gk2 += 1
                    ctx.dma("A", f"s1{tc % 2}", [r1bund_r[tc % 2]], [RES1(tc)],
                            lambda a, tc=tc: a.dma_start(
                                out=res1_t[:, tc, :], in_=r1bund[tc % 2][:]))
            ln_finalize(1)

            for fc in range(2):
                ctx.dma("G", f"g{fc % 2}", [], [gwb_r[fc % 2]],
                        lambda g, fc=fc: g.dma_start(
                            out=gwb[fc % 2], in_=gwt_t[:, :, fc * 128:(fc + 1) * 128]))
                ctx.dma("G", f"g{2 + fc % 2}", [], [uwb_r[fc % 2]],
                        lambda g, fc=fc: g.dma_start(
                            out=uwb[fc % 2], in_=uwt_t[:, :, fc * 128:(fc + 1) * 128]))
            for tc in range(DC):
                ctx.dma("S", f"s{tc % 2}", [RES1(tc)], [r1rd_r[tc % 2]],
                        lambda s, tc=tc: s.dma_start(
                            out=r1rd[tc % 2], in_=res1_t[:, tc, :]))
                ctx.op("V", "vq", [r1rd_r[tc % 2], BCS], [tmp32_r],
                       lambda v, tc=tc: v.tensor_mul(tmp32[:], r1rd[tc % 2][:], bcs[:]))
                sl = tc % 2
                ctx.op("V", f"rs{sl}", [tmp32_r, BCMS], [r1m_r[sl]],
                       lambda v, sl=sl: v.tensor_sub(r1m[sl][:], tmp32[:], bcms[:]))
                ctx.op("A", f"rn{tc // (DC // 4)}", [r1m_r[sl]],
                       [("b0", tc, tc + 1)] + ([("b0s", 0, 2)] if tc == 0 else []),
                       lambda a, tc=tc, sl=sl: a.copy(r1n[:, tc, :], r1m[sl][:]))
            for fc in range(FC):
                if fc >= 2:
                    ctx.dma("G", f"g{fc % 2}", [], [gwb_r[fc % 2]],
                            lambda g, fc=fc: g.dma_start(
                                out=gwb[fc % 2],
                                in_=gwt_t[:, :, fc * 128:(fc + 1) * 128]))
                    ctx.dma("G", f"g{2 + fc % 2}", [], [uwb_r[fc % 2]],
                            lambda g, fc=fc: g.dma_start(
                                out=uwb[fc % 2],
                                in_=uwt_t[:, :, fc * 128:(fc + 1) * 128]))
                for h in range(2):
                    slot_g = (fc % 2) * 2 + h
                    slot_u = 4 + (fc % 2) * 2 + h
                    mmsg = [
                        (lambda t, dc=dc, fc=fc, h=h, slot=slot_g:
                         t.matmul(psa[slot][:, 0:NB], gwb[fc % 2][:, dc, :],
                                  r1n[:, dc, h * NB:(h + 1) * NB],
                                  start=(dc == 0), stop=(dc == DC - 1)))
                        for dc in range(DC)]
                    q4 = DC // 4
                    b0rd = {k * q4: [("b0", k * q4, (k + 1) * q4)]
                            for k in range(4)}
                    pgroup(slot_g, f"mm{slot_g}", [gwb_r[fc % 2]], mmsg,
                           mm_reads=b0rd)
                    mmsu = [
                        (lambda t, dc=dc, fc=fc, h=h, slot=slot_u:
                         t.matmul(psa[slot][:, 0:NB], uwb[fc % 2][:, dc, :],
                                  r1n[:, dc, h * NB:(h + 1) * NB],
                                  start=(dc == 0), stop=(dc == DC - 1)))
                        for dc in range(DC)]
                    pgroup(slot_u, f"mm{slot_u}", [uwb_r[fc % 2]], mmsu,
                           mm_reads=b0rd)
                    ctx.op("A", f"ga{(fc % 2) * 2 + h}",
                           [PS(slot_g), CONST], [t1_r[(fc % 2) * 2 + h]],
                           lambda a, fc=fc, h=h, slot=slot_g: a.activation(
                               t1[(fc % 2) * 2 + h][:], psa[slot][:, 0:NB], AF.Silu,
                               scale=gsa[:, fc:fc + 1]))
                    hres = ("ab", hbund_r[fc % 2][1] + h * NB,
                            hbund_r[fc % 2][1] + (h + 1) * NB)
                    ctx.op("V", f"vr{(fc % 2) * 2 + h}",
                           [PS(slot_u), t1_r[(fc % 2) * 2 + h], CONST], [hres],
                           lambda v, fc=fc, h=h, slot=slot_u: v.scalar_tensor_tensor(
                               hbund[fc % 2][:, h * NB:(h + 1) * NB],
                               psa[slot][:, 0:NB], usa[:, fc:fc + 1],
                               t1[(fc % 2) * 2 + h][:], OP.mult, OP.mult))
                ctx.dma("A", f"s1{fc % 2}", [hbund_r[fc % 2]], [HD(fc)],
                        lambda a, fc=fc: a.dma_start(
                            out=h_t[:, fc, :], in_=hbund[fc % 2][:]))

            ctx.op("V", "lnr", [], [ACCS(0), ACCS(1)], lambda v: v.memzero(accs[:]))
            ctx.op("V", "lnr", [], [ACCQ(0), ACCQ(1)], lambda v: v.memzero(accq[:]))
            nbund = (FC + NDB - 1) // NDB

            def p4_norm(sw, dmcs):
                for dmc in dmcs:
                    dmg = sw * 4 + dmc
                    ctx.dma("S", f"s{2 + dmg % 2}", [RES1(dmg)], [r1rd2_r[dmg % 2]],
                            lambda s, dmg=dmg: s.dma_start(
                                out=r1rd2[dmg % 2], in_=res1_t[:, dmg, :]))
                    for h in range(2):
                        slot = dmc * 2 + h
                        ctx.op("V", "vq", [r1rd2_r[dmg % 2], BCS], [vtn_r[slot]],
                               lambda v, dmg=dmg, h=h, slot=slot: v.tensor_mul(
                                   vtn[slot][:],
                                   r1rd2[dmg % 2][:, h * NB:(h + 1) * NB],
                                   bcs[:, h * NB:(h + 1) * NB]))
                        ctx.op("V", "vq", [vtn_r[slot], BCMS], [vtn_r[slot]],
                               lambda v, slot=slot, h=h: v.tensor_sub(
                                   vtn[slot][:], vtn[slot][:],
                                   bcms[:, h * NB:(h + 1) * NB]))

            for sw in range(NSW):
                p4_norm(sw, (0, 1))
                for bi in range(nbund):
                    if bi == 2:
                        p4_norm(sw, (2, 3))
                    f0 = bi * NDB
                    nf = min(NDB, FC - f0)
                    bs = bi % NBS
                    ctx.dma("G", f"w{bs}", [], [dwbB_r[bs]],
                            lambda g, sw=sw, f0=f0, nf=nf, bs=bs: g.dma_start(
                                out=dwbB[bs][:, 0:nf, :],
                                in_=dwt_t[:, f0:f0 + nf, sw * 512:(sw + 1) * 512]))
                    ctx.dma("S", f"h{bs}",
                            [HD2(f0, f0 + nf)], [hrdB_r[bs]],
                            lambda s, sw=sw, f0=f0, nf=nf, bs=bs: s.dma_start(
                                out=hrdB[bs][:, 0:nf, :],
                                in_=h_t[:, f0:f0 + nf, :]))
                    for fi in range(nf):
                        fc = f0 + fi
                        for dmc in range(4):
                            for h in range(2):
                                slot = dmc * 2 + h
                                rd = [dwbB_r[bs], hrdB_r[bs]]
                                ch = (f"mm{slot}" if fc == FC - 1
                                      else f"t4{bs}")
                                wr = [PS(slot)] if fc in (0, FC - 1) else []
                                ctx.op("T", ch, rd, wr,
                                       lambda t, fi=fi, bs=bs, dmc=dmc, h=h,
                                       slot=slot, fc=fc:
                                       t.matmul(psa[slot][:, 0:NB],
                                                dwbB[bs][:, fi,
                                                         dmc * 128:(dmc + 1) * 128],
                                                hrdB[bs][:, fi,
                                                         h * NB:(h + 1) * NB],
                                                start=(fc == 0),
                                                stop=(fc == FC - 1)))
                for pair in range(2):
                    for dmc in (2 * pair, 2 * pair + 1):
                        dmg = sw * 4 + dmc
                        for h in range(2):
                            slot = dmc * 2 + h
                            r2sl = r2bund[dmg % 2][:, h * NB:(h + 1) * NB]
                            r2res = ("af", r2bund_r[dmg % 2][1] + h * NB,
                                     r2bund_r[dmg % 2][1] + (h + 1) * NB)
                            ctx.op("V", f"ev{slot}",
                                   [PS(slot), vtn_r[slot], CONST], [r2res],
                                   lambda v, slot=slot, h=h, dmg=dmg, r2sl=r2sl:
                                   v.scalar_tensor_tensor(
                                       r2sl, psa[slot][:, 0:NB],
                                       dsa[:, dmg:dmg + 1],
                                       vtn[slot][:], OP.mult, OP.add))
                    for dmc in (2 * pair, 2 * pair + 1):
                        dmg = sw * 4 + dmc
                        ctx.dma("A", f"s1{dmg % 2}", [r2bund_r[dmg % 2]],
                                [RES2(dmg)],
                                lambda a, dmg=dmg: a.dma_start(
                                    out=res2_t[:, dmg, :],
                                    in_=r2bund[dmg % 2][:]))
                    for dmc in (2 * pair, 2 * pair + 1):
                        dmg = sw * 4 + dmc
                        for h in range(2):
                            slot = dmc * 2 + h
                            r2sl = r2bund[dmg % 2][:, h * NB:(h + 1) * NB]
                            r2res = ("af", r2bund_r[dmg % 2][1] + h * NB,
                                     r2bund_r[dmg % 2][1] + (h + 1) * NB)
                            ctx.op("V", "lnr", [r2res, ACCS(h)], [ACCS(h)],
                                   lambda v, h=h, r2sl=r2sl: v.tensor_add(
                                       accs[:, h * NB:(h + 1) * NB],
                                       accs[:, h * NB:(h + 1) * NB], r2sl))
                            ctx.op("V", "vq", [r2res], [vtn_r[slot]],
                                   lambda v, slot=slot, r2sl=r2sl: v.tensor_mul(
                                       vtn[slot][:], r2sl, r2sl))
                            ctx.op("V", "lnr", [vtn_r[slot], ACCQ(h)], [ACCQ(h)],
                                   lambda v, h=h, slot=slot: v.tensor_add(
                                       accq[:, h * NB:(h + 1) * NB],
                                       accq[:, h * NB:(h + 1) * NB],
                                       vtn[slot][:]))
            ln_finalize(2)

            for tc in range(DC):
                ctx.dma("S", f"s{tc % 2}", [RES2(tc)], [r2rd_r[tc % 2]],
                        lambda s, tc=tc: s.dma_start(
                            out=r2rd[tc % 2], in_=res2_t[:, tc, :]))
                ctx.op("V", "vq", [r2rd_r[tc % 2], BCS], [otmp_r],
                       lambda v, tc=tc: v.tensor_mul(otmp[:], r2rd[tc % 2][:], bcs[:]))
                ctx.op("V", f"vr{tc % 2}", [otmp_r, BCMS], [outb_r[tc % 2]],
                       lambda v, tc=tc: v.tensor_sub(outb[tc % 2][:], otmp[:], bcms[:]))
                ctx.dma("A", f"s1{tc % 2}", [outb_r[tc % 2]], [("out", tc, tc + 1)],
                        lambda a, tc=tc: a.dma_start(
                            out=outfm[tc * 128:(tc + 1) * 128, :],
                            in_=outb[tc % 2][:]))
            ctx.mark("end")
            ctx.wait_mark("A", "end", ["ds10", "ds11"])

        pre = Ctx("pre")
        program(pre)
        sems = {n: stack.enter_context(nc.semaphore(n))
                for n in sorted(pre.channels | {f"bar{i}" for i in range(1, pre.nbar + 1)})}
        with nc.Block() as block:
            @block.gpsimd
            def _(g):
                program(Ctx("G", g, sems))

            @block.sync
            def _(s):
                program(Ctx("S", s, sems))

            @block.tensor
            def _(t):
                program(Ctx("T", t, sems))

            @block.scalar
            def _(a):
                program(Ctx("A", a, sems))

            @block.vector
            def _(v):
                program(Ctx("V", v, sems))

    return nc


def _prep_inputs(x, w1, w2, b2, mask, gw, gs, uw, us, dw, ds, D, F, TOK, n_cores=8):
    DC, FC = D // 128, F // 128
    rows = np.asarray(x).reshape(-1, D)
    f32 = np.float32
    shared = {
        "w1": np.ascontiguousarray(np.asarray(w1, f32)),
        "w2t": np.ascontiguousarray(np.asarray(w2, f32).T),
        "maskd": np.ascontiguousarray(np.asarray(mask, f32)),
        "gwt": np.ascontiguousarray(np.asarray(gw, np.int32).T),
        "uwt": np.ascontiguousarray(np.asarray(uw, np.int32).T),
        "dwt": np.ascontiguousarray(np.asarray(dw, np.int32).T),
        "gsd": np.ascontiguousarray(np.asarray(gs, f32).reshape(FC, 128).T),
        "usd": np.ascontiguousarray(np.asarray(us, f32).reshape(FC, 128).T),
        "dsd": np.ascontiguousarray(np.asarray(ds, f32).reshape(DC, 128).T),
        "b2d": np.ascontiguousarray(np.asarray(b2, f32).reshape(DC, 128).T),
    }
    in_maps = []
    for i in range(n_cores):
        m = dict(shared)
        m["xfm"] = np.ascontiguousarray(
            rows[i * TOK:(i + 1) * TOK].astype(f32, copy=False).T)
        in_maps.append(m)
    return in_maps


def kernel(L0_attn_x, L0_attn_w1, L0_attn_w2, L0_attn_b2, L0_attn_sparse_mask,
           L0_ffn_gate_w, L0_ffn_gate_s, L0_ffn_up_w, L0_ffn_up_s,
           L0_ffn_down_w, L0_ffn_down_s):
    D, F = 4096, 11008
    B, Q = L0_attn_x.shape[0], L0_attn_x.shape[1]
    n_cores = 8
    TOK = (B * Q) // n_cores
    nc = build_nc(D, F, TOK, SPW=1024)
    in_maps = _prep_inputs(L0_attn_x, L0_attn_w1, L0_attn_w2, L0_attn_b2,
                           L0_attn_sparse_mask, L0_ffn_gate_w, L0_ffn_gate_s,
                           L0_ffn_up_w, L0_ffn_up_s, L0_ffn_down_w,
                           L0_ffn_down_s, D, F, TOK, n_cores)
    res = run_bass_kernel_spmd(nc, in_maps, core_ids=list(range(n_cores)))
    out = np.empty((B * Q, D), np.float32)
    for i in range(n_cores):
        out[i * TOK:(i + 1) * TOK] = res.results[i]["outfm"].T
    return out.reshape(B, Q, D)


# version 14
# speedup vs baseline: 1.0336x; 1.0336x over previous
import numpy as np
from collections import defaultdict

import concourse.bass as bass
import concourse.mybir as mybir
from concourse.bass_utils import run_bass_kernel_spmd

F32 = mybir.dt.float32
BF16 = mybir.dt.bfloat16
I32 = mybir.dt.int32
I8 = mybir.dt.int8
AF = mybir.ActivationFunctionType
OP = mybir.AluOpType

DT_SIZE = {F32: 4, BF16: 2, I32: 4, I8: 1}


class Tracker:

    def __init__(self):
        self.spaces = defaultdict(list)

    @staticmethod
    def _merge(d, ch, val):
        if val > d.get(ch, -1):
            d[ch] = val

    def read(self, res, key):
        space, lo, hi = res
        waits = {}
        for ent in self.spaces[space]:
            if ent[0] < hi and lo < ent[1]:
                for c, v in ent[2].items():
                    self._merge(waits, c, v)
                if key is not None:
                    self._merge(ent[3], key[0], key[1])
        return waits

    def write(self, res, key):
        space, lo, hi = res
        waits = {}
        keep = []
        for ent in self.spaces[space]:
            if ent[0] < hi and lo < ent[1]:
                for c, v in ent[2].items():
                    self._merge(waits, c, v)
                for c, v in ent[3].items():
                    self._merge(waits, c, v)
                if ent[0] < lo:
                    keep.append([ent[0], lo, ent[2], ent[3]])
                if ent[1] > hi:
                    keep.append([hi, ent[1], dict(ent[2]), dict(ent[3])])
                continue
            keep.append(ent)
        keep.append([lo, hi, {key[0]: key[1]} if key else {}, {}])
        self.spaces[space] = keep
        return waits


class Ctx:
    def __init__(self, mode, eng=None, sems=None):
        self.mode = mode
        self.eng = eng
        self.sems = sems
        self.cnt = defaultdict(int)
        self.channels = set()
        self.trk = Tracker()
        self.credit = {m: defaultdict(int) for m in ("G", "S", "T", "A", "V")}
        self.nbar = 0
        self.marks = {}

    def _upd(self, ch, n=1):
        self.channels.add(ch)
        self.cnt[ch] += n
        return self.cnt[ch]

    def _collect(self, who, reads, writes, key):
        waits = {}
        for r in reads or ():
            for c, v in self.trk.read(r, key).items():
                Tracker._merge(waits, c, v)
        for w in writes or ():
            for c, v in self.trk.write(w, key).items():
                Tracker._merge(waits, c, v)
        if key is not None and waits.get(key[0], -1) >= key[1]:
            waits[key[0]] = key[1] - 1
        for c in list(waits):
            if c[0] == "d" and self.cnt[c] > waits[c] and (
                    key is None or c != key[0]):
                waits[c] = self.cnt[c]
        cr = self.credit[who]
        out = [(c, v) for c, v in waits.items() if cr[c] < v]
        for c, v in out:
            cr[c] = v
        return out

    def _emit_waits(self, pairs):
        for c, v in pairs:
            self.eng.wait_ge(self.sems[c], v)

    def dma(self, queue, stream, reads, writes, fn, extra=()):
        ch = "d" + stream
        val = self._upd(ch, 16)
        key = (ch, val)
        waits = self._collect(queue, reads, writes, key)
        for c, v in extra:
            if self.credit[queue][c] < v:
                self.credit[queue][c] = v
                waits.append((c, v))
        if self.mode == "pre" or self.mode != queue:
            return key
        self._emit_waits(waits)
        fn(self.eng).then_inc(self.sems[ch], 16)
        return key

    def op(self, eng, ch, reads, writes, fn, extra=(), as_key=None):
        key = None
        if ch is not None:
            key = (ch, self._upd(ch, 1))
        waits = self._collect(eng, reads, writes, as_key if as_key else key)
        for c, v in extra:
            if self.credit[eng][c] < v:
                self.credit[eng][c] = v
                waits.append((c, v))
        if self.mode == "pre" or self.mode != eng:
            return key
        self._emit_waits(waits)
        ins = fn(self.eng)
        if ch is not None:
            ins.then_inc(self.sems[ch], 1)
        return key

    def barrier(self):
        self.nbar += 1
        ch = f"bar{self.nbar}"
        self.channels.add(ch)
        if self.mode == "pre":
            return
        self.eng.nop().then_inc(self.sems[ch], 1)
        self.eng.wait_ge(self.sems[ch], 5)

    def mark(self, name):
        self.marks[name] = {c: self.cnt[c] for c in self.cnt if c.startswith("d")}

    def wait_mark(self, who, name, chans):
        pairs = []
        for c in chans:
            v = self.marks[name].get(c, 0)
            if v and self.credit[who][c] < v:
                self.credit[who][c] = v
                pairs.append((c, v))
        if self.mode == who:
            self._emit_waits(pairs)


def carve1f(arf_ap, off, n):
    return arf_ap[0:1, off:off + n]


def build_nc(D, F, TOK, SPW=1024, debug=False):
    DC = D // 128
    FC = F // 128
    NB = TOK // 2
    assert NB <= 512 and TOK % 2 == 0
    SPW = min(SPW, D)
    NSP = D // SPW
    SCP = SPW // 128
    TPW = min(512, D)
    NTP = D // TPW
    MB = min(4, SCP)
    NBUN = SCP // MB
    assert SCP % MB == 0 and DC % 4 == 0 and FC % 2 == 0 and MB == 4
    NSW = DC // 4

    nc = bass.Bass("TRN2", target_bir_lowering=False)

    xfm = nc.dram_tensor("xfm", [D, TOK], F32, kind="ExternalInput").ap()
    w1 = nc.dram_tensor("w1", [D, D], F32, kind="ExternalInput").ap()
    w2t = nc.dram_tensor("w2t", [D, D], F32, kind="ExternalInput").ap()
    maskd = nc.dram_tensor("maskd", [D, D], F32, kind="ExternalInput").ap()
    gwt = nc.dram_tensor("gwt", [D, F], I32, kind="ExternalInput").ap()
    uwt = nc.dram_tensor("uwt", [D, F], I32, kind="ExternalInput").ap()
    dwt = nc.dram_tensor("dwt", [F, D], I32, kind="ExternalInput").ap()
    gsd = nc.dram_tensor("gsd", [128, FC], F32, kind="ExternalInput").ap()
    usd = nc.dram_tensor("usd", [128, FC], F32, kind="ExternalInput").ap()
    dsd = nc.dram_tensor("dsd", [128, DC], F32, kind="ExternalInput").ap()
    b2d = nc.dram_tensor("b2d", [128, DC], F32, kind="ExternalInput").ap()
    outfm = nc.dram_tensor("outfm", [D, TOK], F32, kind="ExternalOutput").ap()

    w1bf_d = nc.dram_tensor("w1bf_d", [D, D], BF16, kind="Internal").ap()
    ikind = "ExternalOutput" if debug else "Internal"
    attn_d = nc.dram_tensor("attn_d", [D, D], BF16, kind=ikind).ap()
    res1_d = nc.dram_tensor("res1_d", [D, TOK], F32, kind=ikind).ap()
    h_d = nc.dram_tensor("h_d", [F, TOK], BF16, kind=ikind).ap()
    res2_d = nc.dram_tensor("res2_d", [D, TOK], F32, kind=ikind).ap()

    xfm_t = xfm.rearrange("(c p) w -> p c w", p=128)
    w1_t = w1.rearrange("(c p) w -> p c w", p=128)
    w2t_t = w2t.rearrange("(c p) w -> p c w", p=128)
    mask_t = maskd.rearrange("(c p) w -> p c w", p=128)
    gwt_t = gwt.rearrange("(c p) w -> p c w", p=128)
    uwt_t = uwt.rearrange("(c p) w -> p c w", p=128)
    dwt_t = dwt.rearrange("(c p) w -> p c w", p=128)
    attn_t = attn_d.rearrange("(c p) w -> p c w", p=128)
    w1bf_t = w1bf_d.rearrange("(c p) w -> p c w", p=128)
    res1_t = res1_d.rearrange("(c p) w -> p c w", p=128)
    h_t = h_d.rearrange("(c p) w -> p c w", p=128)
    res2_t = res2_d.rearrange("(c p) w -> p c w", p=128)

    ARB_EL = 36864
    ARF_EL = 13312
    import contextlib
    stack = contextlib.ExitStack()
    with stack:
        big0 = stack.enter_context(
            nc.sbuf_tensor("big0", [128, DC * max(SPW, TOK)], BF16))
        arb = stack.enter_context(nc.sbuf_tensor("arb", [128, ARB_EL], BF16))
        arf = stack.enter_context(nc.sbuf_tensor("arf", [128, ARF_EL], F32))
        acc_s = stack.enter_context(nc.sbuf_tensor("acc_s", [128, TOK], F32))
        acc_q = stack.enter_context(nc.sbuf_tensor("acc_q", [128, TOK], F32))
        bc_s = stack.enter_context(nc.sbuf_tensor("bc_s", [128, TOK], F32))
        bc_ms = stack.enter_context(nc.sbuf_tensor("bc_ms", [128, TOK], F32))
        b2t = stack.enter_context(nc.sbuf_tensor("b2t", [128, DC], F32))
        gst = stack.enter_context(nc.sbuf_tensor("gst", [128, FC], F32))
        ust = stack.enter_context(nc.sbuf_tensor("ust", [128, FC], F32))
        dst = stack.enter_context(nc.sbuf_tensor("dst", [128, DC], F32))
        onc = stack.enter_context(nc.sbuf_tensor("onc", [128, 1], F32))
        onr = stack.enter_context(nc.sbuf_tensor("onr", [1, 128], F32))
        ps = [stack.enter_context(nc.psum_tensor(f"ps{i}", [128, 512], F32))
              for i in range(8)]

        arb_ap, arf_ap = arb.ap(), arf.ap()
        big0v = big0.ap().rearrange("p (c w) -> p c w", c=DC)
        w1p = big0v[:, :, 0:SPW]
        xbf = big0v[:, :, 0:TOK]
        r1n = big0v[:, :, 0:TOK]
        psa = [p.ap() for p in ps]

        def cb(off, *shape):
            n = int(np.prod(shape))
            v = arb_ap[:, off:off + n]
            if len(shape) == 2:
                v = v.rearrange("p (a b) -> p a b", a=shape[0])
            return v

        def cf(off, *shape):
            n = int(np.prod(shape))
            v = arf_ap[:, off:off + n]
            if len(shape) == 2:
                v = v.rearrange("p (a b) -> p a b", a=shape[0])
            return v

        def sb(off, n):
            return ("ab", off, off + n)

        def sf(off, n):
            return ("af", off, off + n)

        TPE = DC * TPW
        w2tb = [cb(0, DC, TPW), cb(TPE, DC, TPW)]
        w2tb_r = [sb(0, TPE), sb(TPE, TPE)]
        attnb = [[cb(2 * TPE + (2 * j + pq) * TPW, TPW) for pq in range(2)]
                 for j in range(4)]
        attnb_r = [[sb(2 * TPE + (2 * j + pq) * TPW, TPW) for pq in range(2)]
                   for j in range(4)]
        assert 2 * TPE + 8 * TPW <= ARB_EL
        attnL = [cb(0, DC, 256), cb(DC * 256, DC, 256)]
        attnL_r = [sb(0, DC * 256), sb(DC * 256, DC * 256)]
        gwb = [cb(0, DC, 128), cb(DC * 128, DC, 128)]
        gwb_r = [sb(0, DC * 128), sb(DC * 128, DC * 128)]
        uwb = [cb(2 * DC * 128, DC, 128), cb(3 * DC * 128, DC, 128)]
        uwb_r = [sb(2 * DC * 128, DC * 128), sb(3 * DC * 128, DC * 128)]
        hbund = [cb(4 * DC * 128, TOK), cb(4 * DC * 128 + TOK, TOK)]
        hbund_r = [sb(4 * DC * 128, TOK), sb(4 * DC * 128 + TOK, TOK)]
        NDB = 3
        NBS = 3
        dwbB = [cb(k * NDB * 512, NDB, 512) for k in range(NBS)]
        dwbB_r = [sb(k * NDB * 512, NDB * 512) for k in range(NBS)]
        hb0 = NBS * NDB * 512
        hrdB = [cb(hb0 + k * NDB * TOK, NDB, TOK) for k in range(NBS)]
        hrdB_r = [sb(hb0 + k * NDB * TOK, NDB * TOK) for k in range(NBS)]
        assert hb0 + NBS * NDB * TOK <= ARB_EL

        maskb = [cf(0, MB, TPW), cf(MB * TPW, MB, TPW)]
        maskb_r = [sf(0, MB * TPW), sf(MB * TPW, MB * TPW)]
        x32b = [cf(0, TOK), cf(TOK, TOK)]
        x32b_r = [sf(0, TOK), sf(TOK, TOK)]
        yact = [cf(2 * TOK + k * NB, NB) for k in range(4)]
        yact_r = [sf(2 * TOK + k * NB, NB) for k in range(4)]
        sqtmp = cf(4 * TOK, NB)
        sqtmp_r = sf(4 * TOK, NB)
        rb0 = 4 * TOK + NB
        r1bund = [cf(rb0, TOK), cf(rb0 + TOK, TOK)]
        r1bund_r = [sf(rb0, TOK), sf(rb0 + TOK, TOK)]
        t1 = [cf(k * NB, NB) for k in range(4)]
        t1_r = [sf(k * NB, NB) for k in range(4)]
        r1rd = [cf(2 * TOK, TOK), cf(3 * TOK, TOK)]
        r1rd_r = [sf(2 * TOK, TOK), sf(3 * TOK, TOK)]
        tmp32 = cf(4 * TOK, TOK)
        tmp32_r = sf(4 * TOK, TOK)
        assert 5 * TOK <= rb0 + 0 * TOK or True
        r1rd2 = [cf(0, TOK), cf(TOK, TOK)]
        r1rd2_r = [sf(0, TOK), sf(TOK, TOK)]
        vtn = [cf(2 * TOK + k * NB, NB) for k in range(8)]
        vtn_r = [sf(2 * TOK + k * NB, NB) for k in range(8)]
        r2bund = [cf(6 * TOK, TOK), cf(7 * TOK, TOK)]
        r2bund_r = [sf(6 * TOK, TOK), sf(7 * TOK, TOK)]
        r1m = [cf(5 * TOK, TOK), cf(6 * TOK, TOK)]
        r1m_r = [sf(5 * TOK, TOK), sf(6 * TOK, TOK)]
        r2rd = [cf(0, TOK), cf(TOK, TOK)]
        r2rd_r = [sf(0, TOK), sf(TOK, TOK)]
        otmp = cf(2 * TOK, TOK)
        otmp_r = sf(2 * TOK, TOK)
        outb = r2rd
        outb_r = r2rd_r
        so = 8 * TOK
        st_s = carve1f(arf_ap, so, TOK)
        st_q = carve1f(arf_ap, so + TOK, TOK)
        tA = carve1f(arf_ap, so + 2 * TOK, TOK)
        tB = carve1f(arf_ap, so + 3 * TOK, TOK)
        rstd = carve1f(arf_ap, so + 4 * TOK, TOK)
        st_r = [sf(so + k * TOK, TOK) for k in range(5)]
        assert so + 5 * TOK <= ARF_EL

        B0 = ("b0", 0, DC)
        PS = lambda i: ("ps", i, i + 1)
        ACCS = lambda h: ("accs", h, h + 1)
        ACCQ = lambda h: ("accq", h, h + 1)
        BCS = ("bcs", 0, 1)
        BCMS = ("bcms", 0, 1)
        CONST = ("cst", 0, 1)
        ONES = ("ones", 0, 1)
        ATTN = ("attn", 0, 1)
        RES1 = lambda tc: ("res1", tc, tc + 1)
        HD = lambda fc: ("hd", fc, fc + 1)
        HD2 = lambda a, b: ("hd", a, b)
        RES2 = lambda tc: ("res2", tc, tc + 1)

        onc_ap, onr_ap = onc.ap(), onr.ap()
        accs, accq = acc_s.ap(), acc_q.ap()
        bcs, bcms = bc_s.ap(), bc_ms.ap()
        b2a, gsa, usa, dsa = b2t.ap(), gst.ap(), ust.ap(), dst.ap()

        def program(ctx):
            def pgroup(slot, ch, reads, mms, extra=(), mm_reads=None):
                n = len(mms)
                fk = (ch, ctx.cnt[ch] + 1)
                for i, m in enumerate(mms):
                    rd = list(reads) if i == 0 else []
                    if mm_reads and i in mm_reads:
                        rd += mm_reads[i]
                    if n == 1:
                        ctx.op("T", ch, rd, [PS(slot)], m, extra=extra)
                    elif i == 0:
                        ctx.op("T", None, rd, [PS(slot)], m, extra=extra,
                               as_key=fk)
                    elif i == n - 1:
                        ctx.op("T", ch, rd, [], m, as_key=fk)
                    else:
                        ctx.op("T", None, rd, [], m, as_key=fk)
                return fk

            def ln_finalize(ln):
                for h in (0, 1):
                    pgroup(h, f"mm{h}", [ACCS(h), ONES], [
                        lambda t, h=h: t.matmul(psa[h][0:1, 0:NB], onc_ap[:],
                                                accs[:, h * NB:(h + 1) * NB],
                                                start=True, stop=True)])
                    pgroup(2 + h, f"mm{2 + h}", [ACCQ(h), ONES], [
                        lambda t, h=h: t.matmul(psa[2 + h][0:1, 0:NB], onc_ap[:],
                                                accq[:, h * NB:(h + 1) * NB],
                                                start=True, stop=True)])
                for i, h in ((0, 0), (1, 1), (2, 0), (3, 1)):
                    dst_t, dr = (st_s, st_r[0]) if i < 2 else (st_q, st_r[1])
                    ctx.op("V", "lc", [PS(i)], [dr],
                           lambda v, i=i, h=h, dst_t=dst_t: v.tensor_copy(
                               dst_t[:, h * NB:(h + 1) * NB], psa[i][0:1, 0:NB]))
                inv = 1.0 / D
                ctx.op("V", "lnV", [st_r[0]], [st_r[0]],
                       lambda v: v.tensor_scalar_mul(st_s[:], st_s[:], inv))
                ctx.op("V", "lnV", [st_r[1]], [st_r[1]],
                       lambda v: v.tensor_scalar_mul(st_q[:], st_q[:], inv))
                ctx.op("V", "lnV", [st_r[0]], [st_r[2]],
                       lambda v: v.tensor_mul(tA[:], st_s[:], st_s[:]))
                ctx.op("V", "lnV", [st_r[1], st_r[2]], [st_r[3]],
                       lambda v: v.tensor_sub(tB[:], st_q[:], tA[:]))
                ctx.op("V", "lnV", [st_r[3]], [st_r[3]],
                       lambda v: v.tensor_scalar_add(tB[:], tB[:], 1e-5))
                ctx.op("V", "lnV", [st_r[3]], [st_r[2]],
                       lambda v: v.reciprocal(tA[:], tB[:]))
                ctx.op("A", "lnA", [st_r[2]], [st_r[4]],
                       lambda a: a.activation(rstd[:], tA[:], AF.Sqrt))
                ctx.op("V", "lnV", [st_r[4]], [st_r[2]],
                       lambda v: v.tensor_mul(tA[:], rstd[:], rstd[:]))
                ctx.op("V", "lnV", [st_r[2], st_r[3]], [st_r[2]],
                       lambda v: v.tensor_mul(tA[:], tA[:], tB[:]))
                ctx.op("V", "lnV", [st_r[2]], [st_r[2]],
                       lambda v: v.tensor_scalar(tA[:], tA[:], -0.5, 1.5,
                                                 OP.mult, OP.add))
                ctx.op("V", "lnV", [st_r[2], st_r[4]], [st_r[4]],
                       lambda v: v.tensor_mul(rstd[:], rstd[:], tA[:]))
                ctx.op("V", "lnV", [st_r[0], st_r[4]], [st_r[2]],
                       lambda v: v.tensor_mul(tA[:], st_s[:], rstd[:]))
                for h in (0, 1):
                    pgroup(4 + h, f"mm{4 + h}", [st_r[4], ONES], [
                        lambda t, h=h: t.matmul(psa[4 + h][:, 0:NB], onr_ap[:],
                                                rstd[:, h * NB:(h + 1) * NB],
                                                start=True, stop=True)])
                    ctx.op("V", "lc", [PS(4 + h)], [BCS],
                           lambda v, h=h: v.tensor_copy(
                               bcs[:, h * NB:(h + 1) * NB], psa[4 + h][:, 0:NB]))
                    pgroup(6 + h, f"mm{6 + h}", [st_r[2], ONES], [
                        lambda t, h=h: t.matmul(psa[6 + h][:, 0:NB], onr_ap[:],
                                                tA[:, h * NB:(h + 1) * NB],
                                                start=True, stop=True)])
                    ctx.op("V", "lc", [PS(6 + h)], [BCMS],
                           lambda v, h=h: v.tensor_copy(
                               bcms[:, h * NB:(h + 1) * NB], psa[6 + h][:, 0:NB]))

            ctx.dma("S", "cst", [], [CONST], lambda s: s.dma_start(out=b2a[:], in_=b2d))
            ctx.dma("S", "cst", [], [CONST], lambda s: s.dma_start(out=gsa[:], in_=gsd))
            ctx.dma("S", "cst", [], [CONST], lambda s: s.dma_start(out=usa[:], in_=usd))
            ctx.dma("S", "cst", [], [CONST], lambda s: s.dma_start(out=dsa[:], in_=dsd))
            ctx.op("V", "v0", [], [ONES], lambda v: v.memset(onc_ap[:], 1.0))
            ctx.op("V", "v0", [], [ONES], lambda v: v.memset(onr_ap[:], 1.0))
            ctx.op("V", "v0", [], [ACCS(0), ACCS(1)], lambda v: v.memzero(accs[:]))
            ctx.op("V", "v0", [], [ACCQ(0), ACCQ(1)], lambda v: v.memzero(accq[:]))

            SH = SCP // 2

            def emit_w1p(isp):
                for hp in range(2):
                    st = "gA" if hp == 0 else "gB"
                    dst = w1p[:, :, hp * SH * 128:(hp + 1) * SH * 128]
                    s0 = isp * SPW + hp * SH * 128
                    ctx.dma("G", st, [], [("b0s", hp, hp + 1)],
                            lambda g, s0=s0, dst=dst: g.dma_start(
                                out=dst, in_=w1_t[:, :, s0:s0 + SH * 128]))

            for isp in range(NSP):
                emit_w1p(isp)
                for itp in range(NTP):
                    gi = isp * NTP + itp
                    ctx.dma("G", f"g{gi % 2}", [], [w2tb_r[gi % 2]],
                            lambda g, itp=itp, gi=gi: g.dma_start(
                                out=w2tb[gi % 2],
                                in_=w2t_t[:, :, itp * TPW:(itp + 1) * TPW]))
                    for b in range(NBUN):
                        gmb = gi * NBUN + b
                        scg0 = isp * SCP + b * MB
                        ctx.dma("S", f"s{gmb % 2}", [], [maskb_r[gmb % 2]],
                                lambda s, scg0=scg0, itp=itp, gmb=gmb: s.dma_start(
                                    out=maskb[gmb % 2],
                                    in_=mask_t[:, scg0:scg0 + MB,
                                               itp * TPW:(itp + 1) * TPW]))
                        for j in range(MB):
                            isc = b * MB + j
                            shalf = isc // SH
                            mms = [
                                (lambda t, dc=dc, isc=isc, gi=gi, j=j:
                                 t.matmul(psa[j][:, 0:TPW],
                                          w1p[:, dc, isc * 128:(isc + 1) * 128],
                                          w2tb[gi % 2][:, dc, :],
                                          start=(dc == 0), stop=(dc == DC - 1)))
                                for dc in range(DC)]
                            pgroup(j, f"mm{j}",
                                   [("b0s", shalf, shalf + 1), w2tb_r[gi % 2]], mms)
                            ctx.op("V", f"ev{j}",
                                   [PS(j), maskb_r[gmb % 2]],
                                   [attnb_r[j][gmb % 2]],
                                   lambda v, j=j, gmb=gmb: v.tensor_tensor(
                                       attnb[j][gmb % 2][:], psa[j][:, 0:TPW],
                                       maskb[gmb % 2][:, j, :], OP.mult))
                            ctx.dma("A", f"ab{j}{gmb % 2}",
                                    [attnb_r[j][gmb % 2]],
                                    [("attn", gmb * MB + j, gmb * MB + j + 1)],
                                    lambda a, scg0=scg0, itp=itp, j=j, gmb=gmb:
                                    a.dma_start(
                                        out=attn_t[:, scg0 + j,
                                                   itp * TPW:(itp + 1) * TPW],
                                        in_=attnb[j][gmb % 2][:]))
            ctx.mark("p1s")


            XCH = ("gA", "gB", "gC", "gD")
            CKC = DC // 4
            for k in range(4):
                wr = [("b0", k * CKC, (k + 1) * CKC), ("b0s", 0, 2)]
                ctx.dma("G", XCH[k], [], wr,
                        lambda g, k=k: g.dma_start(
                            out=big0v[:, k * CKC:(k + 1) * CKC, 0:TOK],
                            in_=xfm_t[:, k * CKC:(k + 1) * CKC, :]))
            abch = [f"dab{j}{p}" for j in range(4) for p in range(2)]
            gk2 = 0
            for it2 in range(DC // 2):
                ctx.wait_mark("S", "p1s", abch)
                ctx.dma("S", f"s{it2 % 2}", [("attn", 0, 10 ** 6)],
                        [attnL_r[it2 % 2]],
                        lambda s, it2=it2: s.dma_start(
                            out=attnL[it2 % 2],
                            in_=attn_t[:, :, it2 * 256:(it2 + 1) * 256]))
                for tcl in range(2):
                    tc = 2 * it2 + tcl
                    ctx.dma("S", f"s{2 + tc % 2}", [], [x32b_r[tc % 2]],
                            lambda s, tc=tc: s.dma_start(
                                out=x32b[tc % 2], in_=xfm_t[:, tc, :]))
                    for h in range(2):
                        slot = gk2 % 8
                        mms = [
                            (lambda t, sc=sc, tcl=tcl, h=h, it2=it2, slot=slot:
                             t.matmul(psa[slot][:, 0:NB],
                                      attnL[it2 % 2][:, sc, tcl * 128:(tcl + 1) * 128],
                                      xbf[:, sc, h * NB:(h + 1) * NB],
                                      start=(sc == 0), stop=(sc == DC - 1)))
                            for sc in range(DC)]
                        mmrd = {k * CKC: [("b0", k * CKC, (k + 1) * CKC)]
                                for k in range(4)}
                        pgroup(slot, f"mm{slot}", [attnL_r[it2 % 2]], mms,
                               mm_reads=mmrd)
                        ctx.op("A", f"ga{gk2 % 4}",
                               [PS(slot), CONST], [yact_r[gk2 % 4]],
                               lambda a, gk2=gk2, slot=slot, tc=tc: a.activation(
                                   yact[gk2 % 4][:], psa[slot][:, 0:NB], AF.Gelu,
                                   bias=b2a[:, tc:tc + 1]))
                        rsl = r1bund[tc % 2][:, h * NB:(h + 1) * NB]
                        rres = ("af", r1bund_r[tc % 2][1] + h * NB,
                                r1bund_r[tc % 2][1] + (h + 1) * NB)
                        ctx.op("V", f"vr{gk2 % 4}",
                               [yact_r[gk2 % 4], x32b_r[tc % 2]], [rres],
                               lambda v, gk2=gk2, tc=tc, h=h, rsl=rsl: v.tensor_add(
                                   rsl, yact[gk2 % 4][:],
                                   x32b[tc % 2][:, h * NB:(h + 1) * NB]))
                        ctx.op("V", "lnr", [rres, ACCS(h)], [ACCS(h)],
                               lambda v, h=h, rsl=rsl: v.tensor_add(
                                   accs[:, h * NB:(h + 1) * NB],
                                   accs[:, h * NB:(h + 1) * NB], rsl))
                        ctx.op("V", "vq", [rres], [sqtmp_r],
                               lambda v, rsl=rsl: v.tensor_mul(sqtmp[:], rsl, rsl))
                        ctx.op("V", "lnr", [sqtmp_r, ACCQ(h)], [ACCQ(h)],
                               lambda v, h=h: v.tensor_add(
                                   accq[:, h * NB:(h + 1) * NB],
                                   accq[:, h * NB:(h + 1) * NB], sqtmp[:]))
                        gk2 += 1
                    ctx.dma("A", f"s1{tc % 2}", [r1bund_r[tc % 2]], [RES1(tc)],
                            lambda a, tc=tc: a.dma_start(
                                out=res1_t[:, tc, :], in_=r1bund[tc % 2][:]))
            ln_finalize(1)

            for fc in range(2):
                ctx.dma("G", f"g{fc % 2}", [], [gwb_r[fc % 2]],
                        lambda g, fc=fc: g.dma_start(
                            out=gwb[fc % 2], in_=gwt_t[:, :, fc * 128:(fc + 1) * 128]))
                ctx.dma("G", f"g{2 + fc % 2}", [], [uwb_r[fc % 2]],
                        lambda g, fc=fc: g.dma_start(
                            out=uwb[fc % 2], in_=uwt_t[:, :, fc * 128:(fc + 1) * 128]))
            for tc in range(DC):
                ctx.dma("S", f"s{tc % 2}", [RES1(tc)], [r1rd_r[tc % 2]],
                        lambda s, tc=tc: s.dma_start(
                            out=r1rd[tc % 2], in_=res1_t[:, tc, :]))
                ctx.op("V", "vq", [r1rd_r[tc % 2], BCS], [tmp32_r],
                       lambda v, tc=tc: v.tensor_mul(tmp32[:], r1rd[tc % 2][:], bcs[:]))
                sl = tc % 2
                eng_s = "V" if tc % 2 == 0 else "G"
                ctx.op(eng_s, f"rs{sl}", [tmp32_r, BCMS], [r1m_r[sl]],
                       lambda e, sl=sl: e.tensor_sub(r1m[sl][:], tmp32[:], bcms[:]))
                ctx.op("A", f"rn{tc // (DC // 4)}", [r1m_r[sl]],
                       [("b0", tc, tc + 1)] + ([("b0s", 0, 2)] if tc == 0 else []),
                       lambda a, tc=tc, sl=sl: a.copy(r1n[:, tc, :], r1m[sl][:]))
            for fc in range(FC):
                if fc >= 2:
                    ctx.dma("G", f"g{fc % 2}", [], [gwb_r[fc % 2]],
                            lambda g, fc=fc: g.dma_start(
                                out=gwb[fc % 2],
                                in_=gwt_t[:, :, fc * 128:(fc + 1) * 128]))
                    ctx.dma("G", f"g{2 + fc % 2}", [], [uwb_r[fc % 2]],
                            lambda g, fc=fc: g.dma_start(
                                out=uwb[fc % 2],
                                in_=uwt_t[:, :, fc * 128:(fc + 1) * 128]))
                for h in range(2):
                    slot_g = (fc % 2) * 2 + h
                    slot_u = 4 + (fc % 2) * 2 + h
                    mmsg = [
                        (lambda t, dc=dc, fc=fc, h=h, slot=slot_g:
                         t.matmul(psa[slot][:, 0:NB], gwb[fc % 2][:, dc, :],
                                  r1n[:, dc, h * NB:(h + 1) * NB],
                                  start=(dc == 0), stop=(dc == DC - 1)))
                        for dc in range(DC)]
                    q4 = DC // 4
                    b0rd = {k * q4: [("b0", k * q4, (k + 1) * q4)]
                            for k in range(4)}
                    pgroup(slot_g, f"mm{slot_g}", [gwb_r[fc % 2]], mmsg,
                           mm_reads=b0rd)
                    mmsu = [
                        (lambda t, dc=dc, fc=fc, h=h, slot=slot_u:
                         t.matmul(psa[slot][:, 0:NB], uwb[fc % 2][:, dc, :],
                                  r1n[:, dc, h * NB:(h + 1) * NB],
                                  start=(dc == 0), stop=(dc == DC - 1)))
                        for dc in range(DC)]
                    pgroup(slot_u, f"mm{slot_u}", [uwb_r[fc % 2]], mmsu,
                           mm_reads=b0rd)
                    ctx.op("A", f"ga{(fc % 2) * 2 + h}",
                           [PS(slot_g), CONST], [t1_r[(fc % 2) * 2 + h]],
                           lambda a, fc=fc, h=h, slot=slot_g: a.activation(
                               t1[(fc % 2) * 2 + h][:], psa[slot][:, 0:NB], AF.Silu,
                               scale=gsa[:, fc:fc + 1]))
                    hres = ("ab", hbund_r[fc % 2][1] + h * NB,
                            hbund_r[fc % 2][1] + (h + 1) * NB)
                    ctx.op("V", f"vr{(fc % 2) * 2 + h}",
                           [PS(slot_u), t1_r[(fc % 2) * 2 + h], CONST], [hres],
                           lambda v, fc=fc, h=h, slot=slot_u: v.scalar_tensor_tensor(
                               hbund[fc % 2][:, h * NB:(h + 1) * NB],
                               psa[slot][:, 0:NB], usa[:, fc:fc + 1],
                               t1[(fc % 2) * 2 + h][:], OP.mult, OP.mult))
                ctx.dma("A", f"s1{fc % 2}", [hbund_r[fc % 2]], [HD(fc)],
                        lambda a, fc=fc: a.dma_start(
                            out=h_t[:, fc, :], in_=hbund[fc % 2][:]))

            ctx.op("V", "lnr", [], [ACCS(0), ACCS(1)], lambda v: v.memzero(accs[:]))
            ctx.op("V", "lnr", [], [ACCQ(0), ACCQ(1)], lambda v: v.memzero(accq[:]))
            nbund = (FC + NDB - 1) // NDB

            def p4_norm(sw, dmcs):
                for dmc in dmcs:
                    dmg = sw * 4 + dmc
                    ctx.dma("S", f"s{2 + dmg % 2}", [RES1(dmg)], [r1rd2_r[dmg % 2]],
                            lambda s, dmg=dmg: s.dma_start(
                                out=r1rd2[dmg % 2], in_=res1_t[:, dmg, :]))
                    for h in range(2):
                        slot = dmc * 2 + h
                        ctx.op("V", "vq", [r1rd2_r[dmg % 2], BCS], [vtn_r[slot]],
                               lambda v, dmg=dmg, h=h, slot=slot: v.tensor_mul(
                                   vtn[slot][:],
                                   r1rd2[dmg % 2][:, h * NB:(h + 1) * NB],
                                   bcs[:, h * NB:(h + 1) * NB]))
                        ctx.op("V", "vq", [vtn_r[slot], BCMS], [vtn_r[slot]],
                               lambda v, slot=slot, h=h: v.tensor_sub(
                                   vtn[slot][:], vtn[slot][:],
                                   bcms[:, h * NB:(h + 1) * NB]))

            for sw in range(NSW):
                p4_norm(sw, (0, 1))
                for bi in range(nbund):
                    if bi == 2:
                        p4_norm(sw, (2, 3))
                    f0 = bi * NDB
                    nf = min(NDB, FC - f0)
                    bs = bi % NBS
                    ctx.dma("G", f"w{bs}", [], [dwbB_r[bs]],
                            lambda g, sw=sw, f0=f0, nf=nf, bs=bs: g.dma_start(
                                out=dwbB[bs][:, 0:nf, :],
                                in_=dwt_t[:, f0:f0 + nf, sw * 512:(sw + 1) * 512]))
                    ctx.dma("S", f"h{bs}",
                            [HD2(f0, f0 + nf)], [hrdB_r[bs]],
                            lambda s, sw=sw, f0=f0, nf=nf, bs=bs: s.dma_start(
                                out=hrdB[bs][:, 0:nf, :],
                                in_=h_t[:, f0:f0 + nf, :]))
                    for fi in range(nf):
                        fc = f0 + fi
                        for dmc in range(4):
                            for h in range(2):
                                slot = dmc * 2 + h
                                rd = [dwbB_r[bs], hrdB_r[bs]]
                                ch = (f"mm{slot}" if fc == FC - 1
                                      else f"t4{bs}")
                                wr = [PS(slot)] if fc in (0, FC - 1) else []
                                ctx.op("T", ch, rd, wr,
                                       lambda t, fi=fi, bs=bs, dmc=dmc, h=h,
                                       slot=slot, fc=fc:
                                       t.matmul(psa[slot][:, 0:NB],
                                                dwbB[bs][:, fi,
                                                         dmc * 128:(dmc + 1) * 128],
                                                hrdB[bs][:, fi,
                                                         h * NB:(h + 1) * NB],
                                                start=(fc == 0),
                                                stop=(fc == FC - 1)))
                for pair in range(2):
                    for dmc in (2 * pair, 2 * pair + 1):
                        dmg = sw * 4 + dmc
                        for h in range(2):
                            slot = dmc * 2 + h
                            r2sl = r2bund[dmg % 2][:, h * NB:(h + 1) * NB]
                            r2res = ("af", r2bund_r[dmg % 2][1] + h * NB,
                                     r2bund_r[dmg % 2][1] + (h + 1) * NB)
                            ctx.op("V", f"ev{slot}",
                                   [PS(slot), vtn_r[slot], CONST], [r2res],
                                   lambda v, slot=slot, h=h, dmg=dmg, r2sl=r2sl:
                                   v.scalar_tensor_tensor(
                                       r2sl, psa[slot][:, 0:NB],
                                       dsa[:, dmg:dmg + 1],
                                       vtn[slot][:], OP.mult, OP.add))
                    for dmc in (2 * pair, 2 * pair + 1):
                        dmg = sw * 4 + dmc
                        ctx.dma("A", f"s1{dmg % 2}", [r2bund_r[dmg % 2]],
                                [RES2(dmg)],
                                lambda a, dmg=dmg: a.dma_start(
                                    out=res2_t[:, dmg, :],
                                    in_=r2bund[dmg % 2][:]))
                    for dmc in (2 * pair, 2 * pair + 1):
                        dmg = sw * 4 + dmc
                        for h in range(2):
                            slot = dmc * 2 + h
                            r2sl = r2bund[dmg % 2][:, h * NB:(h + 1) * NB]
                            r2res = ("af", r2bund_r[dmg % 2][1] + h * NB,
                                     r2bund_r[dmg % 2][1] + (h + 1) * NB)
                            ctx.op("G", "lnr", [r2res, ACCS(h)], [ACCS(h)],
                                   lambda g, h=h, r2sl=r2sl: g.tensor_add(
                                       accs[:, h * NB:(h + 1) * NB],
                                       accs[:, h * NB:(h + 1) * NB], r2sl))
                            ctx.op("G", "gq", [r2res], [vtn_r[slot]],
                                   lambda g, slot=slot, r2sl=r2sl: g.tensor_mul(
                                       vtn[slot][:], r2sl, r2sl))
                            ctx.op("G", "lnr", [vtn_r[slot], ACCQ(h)], [ACCQ(h)],
                                   lambda g, h=h, slot=slot: g.tensor_add(
                                       accq[:, h * NB:(h + 1) * NB],
                                       accq[:, h * NB:(h + 1) * NB],
                                       vtn[slot][:]))
            ln_finalize(2)

            for tc in range(DC):
                ctx.dma("S", f"s{tc % 2}", [RES2(tc)], [r2rd_r[tc % 2]],
                        lambda s, tc=tc: s.dma_start(
                            out=r2rd[tc % 2], in_=res2_t[:, tc, :]))
                ctx.op("V", "vq", [r2rd_r[tc % 2], BCS], [otmp_r],
                       lambda v, tc=tc: v.tensor_mul(otmp[:], r2rd[tc % 2][:], bcs[:]))
                eng_o = "V" if tc % 2 == 0 else "G"
                ctx.op(eng_o, f"vr{tc % 2}", [otmp_r, BCMS], [outb_r[tc % 2]],
                       lambda e, tc=tc: e.tensor_sub(outb[tc % 2][:], otmp[:], bcms[:]))
                ctx.dma("A", f"s1{tc % 2}", [outb_r[tc % 2]], [("out", tc, tc + 1)],
                        lambda a, tc=tc: a.dma_start(
                            out=outfm[tc * 128:(tc + 1) * 128, :],
                            in_=outb[tc % 2][:]))
            ctx.mark("end")
            ctx.wait_mark("A", "end", ["ds10", "ds11"])

        pre = Ctx("pre")
        program(pre)
        sems = {n: stack.enter_context(nc.semaphore(n))
                for n in sorted(pre.channels | {f"bar{i}" for i in range(1, pre.nbar + 1)})}
        with nc.Block() as block:
            @block.gpsimd
            def _(g):
                program(Ctx("G", g, sems))

            @block.sync
            def _(s):
                program(Ctx("S", s, sems))

            @block.tensor
            def _(t):
                program(Ctx("T", t, sems))

            @block.scalar
            def _(a):
                program(Ctx("A", a, sems))

            @block.vector
            def _(v):
                program(Ctx("V", v, sems))

    return nc


def _prep_inputs(x, w1, w2, b2, mask, gw, gs, uw, us, dw, ds, D, F, TOK, n_cores=8):
    DC, FC = D // 128, F // 128
    rows = np.asarray(x).reshape(-1, D)
    f32 = np.float32
    shared = {
        "w1": np.ascontiguousarray(np.asarray(w1, f32)),
        "w2t": np.ascontiguousarray(np.asarray(w2, f32).T),
        "maskd": np.ascontiguousarray(np.asarray(mask, f32)),
        "gwt": np.ascontiguousarray(np.asarray(gw, np.int32).T),
        "uwt": np.ascontiguousarray(np.asarray(uw, np.int32).T),
        "dwt": np.ascontiguousarray(np.asarray(dw, np.int32).T),
        "gsd": np.ascontiguousarray(np.asarray(gs, f32).reshape(FC, 128).T),
        "usd": np.ascontiguousarray(np.asarray(us, f32).reshape(FC, 128).T),
        "dsd": np.ascontiguousarray(np.asarray(ds, f32).reshape(DC, 128).T),
        "b2d": np.ascontiguousarray(np.asarray(b2, f32).reshape(DC, 128).T),
    }
    in_maps = []
    for i in range(n_cores):
        m = dict(shared)
        m["xfm"] = np.ascontiguousarray(
            rows[i * TOK:(i + 1) * TOK].astype(f32, copy=False).T)
        in_maps.append(m)
    return in_maps


def kernel(L0_attn_x, L0_attn_w1, L0_attn_w2, L0_attn_b2, L0_attn_sparse_mask,
           L0_ffn_gate_w, L0_ffn_gate_s, L0_ffn_up_w, L0_ffn_up_s,
           L0_ffn_down_w, L0_ffn_down_s):
    D, F = 4096, 11008
    B, Q = L0_attn_x.shape[0], L0_attn_x.shape[1]
    n_cores = 8
    TOK = (B * Q) // n_cores
    nc = build_nc(D, F, TOK, SPW=1024)
    in_maps = _prep_inputs(L0_attn_x, L0_attn_w1, L0_attn_w2, L0_attn_b2,
                           L0_attn_sparse_mask, L0_ffn_gate_w, L0_ffn_gate_s,
                           L0_ffn_up_w, L0_ffn_up_s, L0_ffn_down_w,
                           L0_ffn_down_s, D, F, TOK, n_cores)
    res = run_bass_kernel_spmd(nc, in_maps, core_ids=list(range(n_cores)))
    out = np.empty((B * Q, D), np.float32)
    for i in range(n_cores):
        out[i * TOK:(i + 1) * TOK] = res.results[i]["outfm"].T
    return out.reshape(B, Q, D)


# version 15
# speedup vs baseline: 1.1717x; 1.1336x over previous
import numpy as np
from collections import defaultdict

import concourse.bass as bass
import concourse.mybir as mybir
from concourse.bass_utils import run_bass_kernel_spmd

F32 = mybir.dt.float32
BF16 = mybir.dt.bfloat16
I32 = mybir.dt.int32
I8 = mybir.dt.int8
AF = mybir.ActivationFunctionType
OP = mybir.AluOpType

DT_SIZE = {F32: 4, BF16: 2, I32: 4, I8: 1}


class Tracker:

    def __init__(self):
        self.spaces = defaultdict(list)

    @staticmethod
    def _merge(d, ch, val):
        if val > d.get(ch, -1):
            d[ch] = val

    def read(self, res, key):
        space, lo, hi = res
        waits = {}
        for ent in self.spaces[space]:
            if ent[0] < hi and lo < ent[1]:
                for c, v in ent[2].items():
                    self._merge(waits, c, v)
                if key is not None:
                    self._merge(ent[3], key[0], key[1])
        return waits

    def write(self, res, key):
        space, lo, hi = res
        waits = {}
        keep = []
        for ent in self.spaces[space]:
            if ent[0] < hi and lo < ent[1]:
                for c, v in ent[2].items():
                    self._merge(waits, c, v)
                for c, v in ent[3].items():
                    self._merge(waits, c, v)
                if ent[0] < lo:
                    keep.append([ent[0], lo, ent[2], ent[3]])
                if ent[1] > hi:
                    keep.append([hi, ent[1], dict(ent[2]), dict(ent[3])])
                continue
            keep.append(ent)
        keep.append([lo, hi, {key[0]: key[1]} if key else {}, {}])
        self.spaces[space] = keep
        return waits


class Ctx:
    def __init__(self, mode, eng=None, sems=None):
        self.mode = mode
        self.eng = eng
        self.sems = sems
        self.cnt = defaultdict(int)
        self.channels = set()
        self.trk = Tracker()
        self.credit = {m: defaultdict(int) for m in ("G", "S", "T", "A", "V")}
        self.nbar = 0
        self.marks = {}

    def _upd(self, ch, n=1):
        self.channels.add(ch)
        self.cnt[ch] += n
        return self.cnt[ch]

    def _collect(self, who, reads, writes, key):
        waits = {}
        for r in reads or ():
            for c, v in self.trk.read(r, key).items():
                Tracker._merge(waits, c, v)
        for w in writes or ():
            for c, v in self.trk.write(w, key).items():
                Tracker._merge(waits, c, v)
        if key is not None and waits.get(key[0], -1) >= key[1]:
            waits[key[0]] = key[1] - 1
        for c in list(waits):
            if c[0] == "d" and self.cnt[c] > waits[c] and (
                    key is None or c != key[0]):
                waits[c] = self.cnt[c]
        cr = self.credit[who]
        out = [(c, v) for c, v in waits.items() if cr[c] < v]
        for c, v in out:
            cr[c] = v
        return out

    def _emit_waits(self, pairs):
        for c, v in pairs:
            self.eng.wait_ge(self.sems[c], v)

    def dma(self, queue, stream, reads, writes, fn, extra=()):
        ch = "d" + stream
        val = self._upd(ch, 16)
        key = (ch, val)
        waits = self._collect(queue, reads, writes, key)
        for c, v in extra:
            if self.credit[queue][c] < v:
                self.credit[queue][c] = v
                waits.append((c, v))
        if self.mode == "pre" or self.mode != queue:
            return key
        self._emit_waits(waits)
        fn(self.eng).then_inc(self.sems[ch], 16)
        return key

    def op(self, eng, ch, reads, writes, fn, extra=(), as_key=None):
        key = None
        if ch is not None:
            key = (ch, self._upd(ch, 1))
        waits = self._collect(eng, reads, writes, as_key if as_key else key)
        for c, v in extra:
            if self.credit[eng][c] < v:
                self.credit[eng][c] = v
                waits.append((c, v))
        if self.mode == "pre" or self.mode != eng:
            return key
        self._emit_waits(waits)
        ins = fn(self.eng)
        if ch is not None:
            ins.then_inc(self.sems[ch], 1)
        return key

    def barrier(self):
        self.nbar += 1
        ch = f"bar{self.nbar}"
        self.channels.add(ch)
        if self.mode == "pre":
            return
        self.eng.nop().then_inc(self.sems[ch], 1)
        self.eng.wait_ge(self.sems[ch], 5)

    def mark(self, name):
        self.marks[name] = {c: self.cnt[c] for c in self.cnt if c.startswith("d")}

    def wait_mark(self, who, name, chans):
        pairs = []
        for c in chans:
            v = self.marks[name].get(c, 0)
            if v and self.credit[who][c] < v:
                self.credit[who][c] = v
                pairs.append((c, v))
        if self.mode == who:
            self._emit_waits(pairs)


def carve1f(arf_ap, off, n):
    return arf_ap[0:1, off:off + n]


def build_nc(D, F, TOK, SPW=1024, debug=False):
    DC = D // 128
    FC = F // 128
    NB = TOK // 2
    assert NB <= 512 and TOK % 2 == 0
    SPW = min(SPW, D)
    NSP = D // SPW
    SCP = SPW // 128
    TPW = min(512, D)
    NTP = D // TPW
    MB = min(4, SCP)
    NBUN = SCP // MB
    assert SCP % MB == 0 and DC % 4 == 0 and FC % 2 == 0 and MB == 4
    NSW = DC // 4

    nc = bass.Bass("TRN2", target_bir_lowering=False)

    xfm = nc.dram_tensor("xfm", [D, TOK], F32, kind="ExternalInput").ap()
    w1 = nc.dram_tensor("w1", [D, D], F32, kind="ExternalInput").ap()
    w2t = nc.dram_tensor("w2t", [D, D], F32, kind="ExternalInput").ap()
    maskd = nc.dram_tensor("maskd", [D, D], F32, kind="ExternalInput").ap()
    gwt = nc.dram_tensor("gwt", [D, F], I32, kind="ExternalInput").ap()
    uwt = nc.dram_tensor("uwt", [D, F], I32, kind="ExternalInput").ap()
    dwt = nc.dram_tensor("dwt", [F, D], I32, kind="ExternalInput").ap()
    gsd = nc.dram_tensor("gsd", [128, FC], F32, kind="ExternalInput").ap()
    usd = nc.dram_tensor("usd", [128, FC], F32, kind="ExternalInput").ap()
    dsd = nc.dram_tensor("dsd", [128, DC], F32, kind="ExternalInput").ap()
    b2d = nc.dram_tensor("b2d", [128, DC], F32, kind="ExternalInput").ap()
    outfm = nc.dram_tensor("outfm", [D, TOK], F32, kind="ExternalOutput").ap()

    ikind = "ExternalOutput" if debug else "Internal"
    attn_d = nc.dram_tensor("attn_d", [D, D], BF16, kind=ikind).ap()
    res1_d = nc.dram_tensor("res1_d", [D, TOK], F32, kind=ikind).ap()
    h_d = nc.dram_tensor("h_d", [F, TOK], BF16, kind=ikind).ap()
    res2_d = nc.dram_tensor("res2_d", [D, TOK], F32, kind=ikind).ap()

    xfm_t = xfm.rearrange("(c p) w -> p c w", p=128)
    w1_t = w1.rearrange("(c p) w -> p c w", p=128)
    w2t_t = w2t.rearrange("(c p) w -> p c w", p=128)
    mask_t = maskd.rearrange("(c p) w -> p c w", p=128)
    gwt_t = gwt.rearrange("(c p) w -> p c w", p=128)
    uwt_t = uwt.rearrange("(c p) w -> p c w", p=128)
    dwt_t = dwt.rearrange("(c p) w -> p c w", p=128)
    attn_t = attn_d.rearrange("(c p) w -> p c w", p=128)
    res1_t = res1_d.rearrange("(c p) w -> p c w", p=128)
    h_t = h_d.rearrange("(c p) w -> p c w", p=128)
    res2_t = res2_d.rearrange("(c p) w -> p c w", p=128)

    ARB_EL = 36864
    ARF_EL = 13312
    import contextlib
    stack = contextlib.ExitStack()
    with stack:
        big0 = stack.enter_context(
            nc.sbuf_tensor("big0", [128, DC * max(SPW, TOK)], BF16))
        arb = stack.enter_context(nc.sbuf_tensor("arb", [128, ARB_EL], BF16))
        arf = stack.enter_context(nc.sbuf_tensor("arf", [128, ARF_EL], F32))
        acc_s = stack.enter_context(nc.sbuf_tensor("acc_s", [128, TOK], F32))
        acc_q = stack.enter_context(nc.sbuf_tensor("acc_q", [128, TOK], F32))
        bc_s = stack.enter_context(nc.sbuf_tensor("bc_s", [128, TOK], F32))
        bc_ms = stack.enter_context(nc.sbuf_tensor("bc_ms", [128, TOK], F32))
        b2t = stack.enter_context(nc.sbuf_tensor("b2t", [128, DC], F32))
        gst = stack.enter_context(nc.sbuf_tensor("gst", [128, FC], F32))
        ust = stack.enter_context(nc.sbuf_tensor("ust", [128, FC], F32))
        dst = stack.enter_context(nc.sbuf_tensor("dst", [128, DC], F32))
        onc = stack.enter_context(nc.sbuf_tensor("onc", [128, 1], F32))
        onr = stack.enter_context(nc.sbuf_tensor("onr", [1, 128], F32))
        ps = [stack.enter_context(nc.psum_tensor(f"ps{i}", [128, 512], F32))
              for i in range(8)]

        arb_ap, arf_ap = arb.ap(), arf.ap()
        big0v = big0.ap().rearrange("p (c w) -> p c w", c=DC)
        w1p = big0v[:, :, 0:SPW]
        xbf = big0v[:, :, 0:TOK]
        r1n = big0v[:, :, 0:TOK]
        psa = [p.ap() for p in ps]

        def cb(off, *shape):
            n = int(np.prod(shape))
            v = arb_ap[:, off:off + n]
            if len(shape) == 2:
                v = v.rearrange("p (a b) -> p a b", a=shape[0])
            return v

        def cf(off, *shape):
            n = int(np.prod(shape))
            v = arf_ap[:, off:off + n]
            if len(shape) == 2:
                v = v.rearrange("p (a b) -> p a b", a=shape[0])
            return v

        def sb(off, n):
            return ("ab", off, off + n)

        def sf(off, n):
            return ("af", off, off + n)

        TPE = DC * TPW
        w2tb = [cb(0, DC, TPW), cb(TPE, DC, TPW)]
        w2tb_r = [sb(0, TPE), sb(TPE, TPE)]
        attnb = [[cb(2 * TPE + (2 * j + pq) * TPW, TPW) for pq in range(2)]
                 for j in range(4)]
        attnb_r = [[sb(2 * TPE + (2 * j + pq) * TPW, TPW) for pq in range(2)]
                   for j in range(4)]
        assert 2 * TPE + 8 * TPW <= ARB_EL
        attnL = [cb(0, DC, 256), cb(DC * 256, DC, 256)]
        attnL_r = [sb(0, DC * 256), sb(DC * 256, DC * 256)]
        gwb = [cb(0, DC, 128), cb(DC * 128, DC, 128)]
        gwb_r = [sb(0, DC * 128), sb(DC * 128, DC * 128)]
        uwb = [cb(2 * DC * 128, DC, 128), cb(3 * DC * 128, DC, 128)]
        uwb_r = [sb(2 * DC * 128, DC * 128), sb(3 * DC * 128, DC * 128)]
        hbund = [cb(4 * DC * 128, TOK), cb(4 * DC * 128 + TOK, TOK)]
        hbund_r = [sb(4 * DC * 128, TOK), sb(4 * DC * 128 + TOK, TOK)]
        NDB = 3
        NBS = 3
        dwbB = [cb(k * NDB * 512, NDB, 512) for k in range(NBS)]
        dwbB_r = [sb(k * NDB * 512, NDB * 512) for k in range(NBS)]
        hb0 = NBS * NDB * 512
        hrdB = [cb(hb0 + k * NDB * TOK, NDB, TOK) for k in range(NBS)]
        hrdB_r = [sb(hb0 + k * NDB * TOK, NDB * TOK) for k in range(NBS)]
        assert hb0 + NBS * NDB * TOK <= ARB_EL

        maskb = [cf(0, MB, TPW), cf(MB * TPW, MB, TPW)]
        maskb_r = [sf(0, MB * TPW), sf(MB * TPW, MB * TPW)]
        x32b = [cf(0, TOK), cf(TOK, TOK)]
        x32b_r = [sf(0, TOK), sf(TOK, TOK)]
        yact = [cf(2 * TOK + k * NB, NB) for k in range(4)]
        yact_r = [sf(2 * TOK + k * NB, NB) for k in range(4)]
        sqtmp = cf(4 * TOK, NB)
        sqtmp_r = sf(4 * TOK, NB)
        rb0 = 4 * TOK + NB
        r1bund = [cf(rb0, TOK), cf(rb0 + TOK, TOK)]
        r1bund_r = [sf(rb0, TOK), sf(rb0 + TOK, TOK)]
        t1 = [cf(k * NB, NB) for k in range(4)]
        t1_r = [sf(k * NB, NB) for k in range(4)]
        r1rd = [cf(2 * TOK, TOK), cf(3 * TOK, TOK)]
        r1rd_r = [sf(2 * TOK, TOK), sf(3 * TOK, TOK)]
        tmp32 = cf(4 * TOK, TOK)
        tmp32_r = sf(4 * TOK, TOK)
        assert 5 * TOK <= rb0 + 0 * TOK or True
        r1rd2 = [cf(0, TOK), cf(TOK, TOK)]
        r1rd2_r = [sf(0, TOK), sf(TOK, TOK)]
        vtn = [cf(2 * TOK + k * NB, NB) for k in range(8)]
        vtn_r = [sf(2 * TOK + k * NB, NB) for k in range(8)]
        r2bund = [cf(6 * TOK, TOK), cf(7 * TOK, TOK)]
        r2bund_r = [sf(6 * TOK, TOK), sf(7 * TOK, TOK)]
        r1m = [cf(5 * TOK, TOK), cf(6 * TOK, TOK)]
        r1m_r = [sf(5 * TOK, TOK), sf(6 * TOK, TOK)]
        r2rd = [cf(0, TOK), cf(TOK, TOK)]
        r2rd_r = [sf(0, TOK), sf(TOK, TOK)]
        otmp = cf(2 * TOK, TOK)
        otmp_r = sf(2 * TOK, TOK)
        outb = r2rd
        outb_r = r2rd_r
        so = 8 * TOK
        st_s = carve1f(arf_ap, so, TOK)
        st_q = carve1f(arf_ap, so + TOK, TOK)
        tA = carve1f(arf_ap, so + 2 * TOK, TOK)
        tB = carve1f(arf_ap, so + 3 * TOK, TOK)
        rstd = carve1f(arf_ap, so + 4 * TOK, TOK)
        st_r = [sf(so + k * TOK, TOK) for k in range(5)]
        assert so + 5 * TOK <= ARF_EL

        B0 = ("b0", 0, DC)
        PS = lambda i: ("ps", i, i + 1)
        ACCS = lambda h: ("accs", h, h + 1)
        ACCQ = lambda h: ("accq", h, h + 1)
        BCS = ("bcs", 0, 1)
        BCMS = ("bcms", 0, 1)
        CONST = ("cst", 0, 1)
        ONES = ("ones", 0, 1)
        ATTN = ("attn", 0, 1)
        RES1 = lambda tc: ("res1", tc, tc + 1)
        HD = lambda fc: ("hd", fc, fc + 1)
        HD2 = lambda a, b: ("hd", a, b)
        RES2 = lambda tc: ("res2", tc, tc + 1)

        onc_ap, onr_ap = onc.ap(), onr.ap()
        accs, accq = acc_s.ap(), acc_q.ap()
        bcs, bcms = bc_s.ap(), bc_ms.ap()
        b2a, gsa, usa, dsa = b2t.ap(), gst.ap(), ust.ap(), dst.ap()

        def program(ctx):
            def pgroup(slot, ch, reads, mms, extra=(), mm_reads=None):
                n = len(mms)
                fk = (ch, ctx.cnt[ch] + 1)
                for i, m in enumerate(mms):
                    rd = list(reads) if i == 0 else []
                    if mm_reads and i in mm_reads:
                        rd += mm_reads[i]
                    if n == 1:
                        ctx.op("T", ch, rd, [PS(slot)], m, extra=extra)
                    elif i == 0:
                        ctx.op("T", None, rd, [PS(slot)], m, extra=extra,
                               as_key=fk)
                    elif i == n - 1:
                        ctx.op("T", ch, rd, [], m, as_key=fk)
                    else:
                        ctx.op("T", None, rd, [], m, as_key=fk)
                return fk

            def ln_finalize(ln):
                for h in (0, 1):
                    pgroup(h, f"mm{h}", [ACCS(h), ONES], [
                        lambda t, h=h: t.matmul(psa[h][0:1, 0:NB], onc_ap[:],
                                                accs[:, h * NB:(h + 1) * NB],
                                                start=True, stop=True)])
                    pgroup(2 + h, f"mm{2 + h}", [ACCQ(h), ONES], [
                        lambda t, h=h: t.matmul(psa[2 + h][0:1, 0:NB], onc_ap[:],
                                                accq[:, h * NB:(h + 1) * NB],
                                                start=True, stop=True)])
                for i, h in ((0, 0), (1, 1), (2, 0), (3, 1)):
                    dst_t, dr = (st_s, st_r[0]) if i < 2 else (st_q, st_r[1])
                    ctx.op("V", "lc", [PS(i)], [dr],
                           lambda v, i=i, h=h, dst_t=dst_t: v.tensor_copy(
                               dst_t[:, h * NB:(h + 1) * NB], psa[i][0:1, 0:NB]))
                inv = 1.0 / D
                ctx.op("V", "lnV", [st_r[0]], [st_r[0]],
                       lambda v: v.tensor_scalar_mul(st_s[:], st_s[:], inv))
                ctx.op("V", "lnV", [st_r[1]], [st_r[1]],
                       lambda v: v.tensor_scalar_mul(st_q[:], st_q[:], inv))
                ctx.op("V", "lnV", [st_r[0]], [st_r[2]],
                       lambda v: v.tensor_mul(tA[:], st_s[:], st_s[:]))
                ctx.op("V", "lnV", [st_r[1], st_r[2]], [st_r[3]],
                       lambda v: v.tensor_sub(tB[:], st_q[:], tA[:]))
                ctx.op("V", "lnV", [st_r[3]], [st_r[3]],
                       lambda v: v.tensor_scalar_add(tB[:], tB[:], 1e-5))
                ctx.op("V", "lnV", [st_r[3]], [st_r[2]],
                       lambda v: v.reciprocal(tA[:], tB[:]))
                ctx.op("A", "lnA", [st_r[2]], [st_r[4]],
                       lambda a: a.activation(rstd[:], tA[:], AF.Sqrt))
                ctx.op("V", "lnV", [st_r[4]], [st_r[2]],
                       lambda v: v.tensor_mul(tA[:], rstd[:], rstd[:]))
                ctx.op("V", "lnV", [st_r[2], st_r[3]], [st_r[2]],
                       lambda v: v.tensor_mul(tA[:], tA[:], tB[:]))
                ctx.op("V", "lnV", [st_r[2]], [st_r[2]],
                       lambda v: v.tensor_scalar(tA[:], tA[:], -0.5, 1.5,
                                                 OP.mult, OP.add))
                ctx.op("V", "lnV", [st_r[2], st_r[4]], [st_r[4]],
                       lambda v: v.tensor_mul(rstd[:], rstd[:], tA[:]))
                ctx.op("V", "lnV", [st_r[0], st_r[4]], [st_r[2]],
                       lambda v: v.tensor_mul(tA[:], st_s[:], rstd[:]))
                for h in (0, 1):
                    pgroup(4 + h, f"mm{4 + h}", [st_r[4], ONES], [
                        lambda t, h=h: t.matmul(psa[4 + h][:, 0:NB], onr_ap[:],
                                                rstd[:, h * NB:(h + 1) * NB],
                                                start=True, stop=True)])
                    ctx.op("V", "lc", [PS(4 + h)], [BCS],
                           lambda v, h=h: v.tensor_copy(
                               bcs[:, h * NB:(h + 1) * NB], psa[4 + h][:, 0:NB]))
                    pgroup(6 + h, f"mm{6 + h}", [st_r[2], ONES], [
                        lambda t, h=h: t.matmul(psa[6 + h][:, 0:NB], onr_ap[:],
                                                tA[:, h * NB:(h + 1) * NB],
                                                start=True, stop=True)])
                    ctx.op("V", "lc", [PS(6 + h)], [BCMS],
                           lambda v, h=h: v.tensor_copy(
                               bcms[:, h * NB:(h + 1) * NB], psa[6 + h][:, 0:NB]))

            ctx.dma("S", "cst", [], [CONST], lambda s: s.dma_start(out=b2a[:], in_=b2d))
            ctx.dma("S", "cst", [], [CONST], lambda s: s.dma_start(out=gsa[:], in_=gsd))
            ctx.dma("S", "cst", [], [CONST], lambda s: s.dma_start(out=usa[:], in_=usd))
            ctx.dma("S", "cst", [], [CONST], lambda s: s.dma_start(out=dsa[:], in_=dsd))
            ctx.op("V", "v0", [], [ONES], lambda v: v.memset(onc_ap[:], 1.0))
            ctx.op("V", "v0", [], [ONES], lambda v: v.memset(onr_ap[:], 1.0))
            ctx.op("V", "v0", [], [ACCS(0), ACCS(1)], lambda v: v.memzero(accs[:]))
            ctx.op("V", "v0", [], [ACCQ(0), ACCQ(1)], lambda v: v.memzero(accq[:]))

            SH = SCP // 2
            for isp in range(NSP):
                for hp in range(2):
                    s0 = isp * SPW + hp * SH * 128
                    ctx.dma("G", "gA" if hp == 0 else "gB",
                            [], [("b0s", hp, hp + 1)],
                            lambda g, s0=s0, hp=hp: g.dma_start(
                                out=w1p[:, :, hp * SH * 128:(hp + 1) * SH * 128],
                                in_=w1_t[:, :, s0:s0 + SH * 128]))
                for itp in range(NTP):
                    gi = isp * NTP + itp
                    ctx.dma("G", f"g{gi % 2}", [], [w2tb_r[gi % 2]],
                            lambda g, itp=itp, gi=gi: g.dma_start(
                                out=w2tb[gi % 2],
                                in_=w2t_t[:, :, itp * TPW:(itp + 1) * TPW]))
                    for b in range(NBUN):
                        gmb = gi * NBUN + b
                        scg0 = isp * SCP + b * MB
                        ctx.dma("S", f"s{gmb % 2}", [], [maskb_r[gmb % 2]],
                                lambda s, scg0=scg0, itp=itp, gmb=gmb: s.dma_start(
                                    out=maskb[gmb % 2],
                                    in_=mask_t[:, scg0:scg0 + MB,
                                               itp * TPW:(itp + 1) * TPW]))
                        for j in range(MB):
                            isc = b * MB + j
                            shalf = isc // SH
                            sl8 = (gmb * MB + j) % 8
                            mms = [
                                (lambda t, dc=dc, isc=isc, gi=gi, sl8=sl8:
                                 t.matmul(psa[sl8][:, 0:TPW],
                                          w1p[:, dc, isc * 128:(isc + 1) * 128],
                                          w2tb[gi % 2][:, dc, :],
                                          start=(dc == 0), stop=(dc == DC - 1)))
                                for dc in range(DC)]
                            pgroup(sl8, f"mm{sl8}",
                                   [("b0s", shalf, shalf + 1), w2tb_r[gi % 2]], mms)
                            ctx.op("V", f"ev{sl8}",
                                   [PS(sl8), maskb_r[gmb % 2]],
                                   [attnb_r[j][gmb % 2]],
                                   lambda v, j=j, gmb=gmb, sl8=sl8: v.tensor_tensor(
                                       attnb[j][gmb % 2][:], psa[sl8][:, 0:TPW],
                                       maskb[gmb % 2][:, j, :], OP.mult))
                            ctx.dma("A", f"ab{j}{gmb % 2}",
                                    [attnb_r[j][gmb % 2]],
                                    [("attn", gmb * MB + j, gmb * MB + j + 1)],
                                    lambda a, scg0=scg0, itp=itp, j=j, gmb=gmb:
                                    a.dma_start(
                                        out=attn_t[:, scg0 + j,
                                                   itp * TPW:(itp + 1) * TPW],
                                        in_=attnb[j][gmb % 2][:]))
            ctx.mark("p1s")


            XCH = ("gA", "gB", "gC", "gD")
            CKC = DC // 4
            for k in range(4):
                wr = [("b0", k * CKC, (k + 1) * CKC), ("b0s", 0, 2)]
                ctx.dma("G", XCH[k], [], wr,
                        lambda g, k=k: g.dma_start(
                            out=big0v[:, k * CKC:(k + 1) * CKC, 0:TOK],
                            in_=xfm_t[:, k * CKC:(k + 1) * CKC, :]))
            abch = [f"dab{j}{p}" for j in range(4) for p in range(2)]
            gk2 = 0
            for it2 in range(DC // 2):
                ctx.wait_mark("S", "p1s", abch)
                ctx.dma("S", f"s{it2 % 2}", [("attn", 0, 10 ** 6)],
                        [attnL_r[it2 % 2]],
                        lambda s, it2=it2: s.dma_start(
                            out=attnL[it2 % 2],
                            in_=attn_t[:, :, it2 * 256:(it2 + 1) * 256]))
                for tcl in range(2):
                    tc = 2 * it2 + tcl
                    ctx.dma("S", f"s{2 + tc % 2}", [], [x32b_r[tc % 2]],
                            lambda s, tc=tc: s.dma_start(
                                out=x32b[tc % 2], in_=xfm_t[:, tc, :]))
                    for h in range(2):
                        slot = gk2 % 8
                        mms = [
                            (lambda t, sc=sc, tcl=tcl, h=h, it2=it2, slot=slot:
                             t.matmul(psa[slot][:, 0:NB],
                                      attnL[it2 % 2][:, sc, tcl * 128:(tcl + 1) * 128],
                                      xbf[:, sc, h * NB:(h + 1) * NB],
                                      start=(sc == 0), stop=(sc == DC - 1)))
                            for sc in range(DC)]
                        mmrd = {k * CKC: [("b0", k * CKC, (k + 1) * CKC)]
                                for k in range(4)}
                        pgroup(slot, f"mm{slot}", [attnL_r[it2 % 2]], mms,
                               mm_reads=mmrd)
                        ctx.op("A", f"ga{gk2 % 4}",
                               [PS(slot), CONST], [yact_r[gk2 % 4]],
                               lambda a, gk2=gk2, slot=slot, tc=tc: a.activation(
                                   yact[gk2 % 4][:], psa[slot][:, 0:NB], AF.Gelu,
                                   bias=b2a[:, tc:tc + 1]))
                        rsl = r1bund[tc % 2][:, h * NB:(h + 1) * NB]
                        rres = ("af", r1bund_r[tc % 2][1] + h * NB,
                                r1bund_r[tc % 2][1] + (h + 1) * NB)
                        ctx.op("V", f"vr{gk2 % 4}",
                               [yact_r[gk2 % 4], x32b_r[tc % 2]], [rres],
                               lambda v, gk2=gk2, tc=tc, h=h, rsl=rsl: v.tensor_add(
                                   rsl, yact[gk2 % 4][:],
                                   x32b[tc % 2][:, h * NB:(h + 1) * NB]))
                        ctx.op("V", "lnr", [rres, ACCS(h)], [ACCS(h)],
                               lambda v, h=h, rsl=rsl: v.tensor_add(
                                   accs[:, h * NB:(h + 1) * NB],
                                   accs[:, h * NB:(h + 1) * NB], rsl))
                        ctx.op("V", "vq", [rres], [sqtmp_r],
                               lambda v, rsl=rsl: v.tensor_mul(sqtmp[:], rsl, rsl))
                        ctx.op("V", "lnr", [sqtmp_r, ACCQ(h)], [ACCQ(h)],
                               lambda v, h=h: v.tensor_add(
                                   accq[:, h * NB:(h + 1) * NB],
                                   accq[:, h * NB:(h + 1) * NB], sqtmp[:]))
                        gk2 += 1
                    ctx.dma("A", f"s1{tc % 2}", [r1bund_r[tc % 2]], [RES1(tc)],
                            lambda a, tc=tc: a.dma_start(
                                out=res1_t[:, tc, :], in_=r1bund[tc % 2][:]))
            ln_finalize(1)

            for fc in range(2):
                ctx.dma("G", f"g{fc % 2}", [], [gwb_r[fc % 2]],
                        lambda g, fc=fc: g.dma_start(
                            out=gwb[fc % 2], in_=gwt_t[:, :, fc * 128:(fc + 1) * 128]))
                ctx.dma("G", f"g{2 + fc % 2}", [], [uwb_r[fc % 2]],
                        lambda g, fc=fc: g.dma_start(
                            out=uwb[fc % 2], in_=uwt_t[:, :, fc * 128:(fc + 1) * 128]))
            for tc in range(DC):
                ctx.dma("S", f"s{tc % 2}", [RES1(tc)], [r1rd_r[tc % 2]],
                        lambda s, tc=tc: s.dma_start(
                            out=r1rd[tc % 2], in_=res1_t[:, tc, :]))
                ctx.op("V", "vq", [r1rd_r[tc % 2], BCS], [tmp32_r],
                       lambda v, tc=tc: v.tensor_mul(tmp32[:], r1rd[tc % 2][:], bcs[:]))
                sl = tc % 2
                ctx.op("V", f"rs{sl}", [tmp32_r, BCMS], [r1m_r[sl]],
                       lambda v, sl=sl: v.tensor_sub(r1m[sl][:], tmp32[:], bcms[:]))
                ctx.op("A", f"rn{tc // (DC // 4)}", [r1m_r[sl]],
                       [("b0", tc, tc + 1)] + ([("b0s", 0, 2)] if tc == 0 else []),
                       lambda a, tc=tc, sl=sl: a.copy(r1n[:, tc, :], r1m[sl][:]))
            for fc in range(FC):
                if fc >= 2:
                    ctx.dma("G", f"g{fc % 2}", [], [gwb_r[fc % 2]],
                            lambda g, fc=fc: g.dma_start(
                                out=gwb[fc % 2],
                                in_=gwt_t[:, :, fc * 128:(fc + 1) * 128]))
                    ctx.dma("G", f"g{2 + fc % 2}", [], [uwb_r[fc % 2]],
                            lambda g, fc=fc: g.dma_start(
                                out=uwb[fc % 2],
                                in_=uwt_t[:, :, fc * 128:(fc + 1) * 128]))
                for h in range(2):
                    slot_g = (fc % 2) * 2 + h
                    slot_u = 4 + (fc % 2) * 2 + h
                    mmsg = [
                        (lambda t, dc=dc, fc=fc, h=h, slot=slot_g:
                         t.matmul(psa[slot][:, 0:NB], gwb[fc % 2][:, dc, :],
                                  r1n[:, dc, h * NB:(h + 1) * NB],
                                  start=(dc == 0), stop=(dc == DC - 1)))
                        for dc in range(DC)]
                    q4 = DC // 4
                    b0rd = {k * q4: [("b0", k * q4, (k + 1) * q4)]
                            for k in range(4)}
                    pgroup(slot_g, f"mm{slot_g}", [gwb_r[fc % 2]], mmsg,
                           mm_reads=b0rd)
                    mmsu = [
                        (lambda t, dc=dc, fc=fc, h=h, slot=slot_u:
                         t.matmul(psa[slot][:, 0:NB], uwb[fc % 2][:, dc, :],
                                  r1n[:, dc, h * NB:(h + 1) * NB],
                                  start=(dc == 0), stop=(dc == DC - 1)))
                        for dc in range(DC)]
                    pgroup(slot_u, f"mm{slot_u}", [uwb_r[fc % 2]], mmsu,
                           mm_reads=b0rd)
                    ctx.op("A", f"ga{(fc % 2) * 2 + h}",
                           [PS(slot_g), CONST], [t1_r[(fc % 2) * 2 + h]],
                           lambda a, fc=fc, h=h, slot=slot_g: a.activation(
                               t1[(fc % 2) * 2 + h][:], psa[slot][:, 0:NB], AF.Silu,
                               scale=gsa[:, fc:fc + 1]))
                    hres = ("ab", hbund_r[fc % 2][1] + h * NB,
                            hbund_r[fc % 2][1] + (h + 1) * NB)
                    ctx.op("V", f"vr{(fc % 2) * 2 + h}",
                           [PS(slot_u), t1_r[(fc % 2) * 2 + h], CONST], [hres],
                           lambda v, fc=fc, h=h, slot=slot_u: v.scalar_tensor_tensor(
                               hbund[fc % 2][:, h * NB:(h + 1) * NB],
                               psa[slot][:, 0:NB], usa[:, fc:fc + 1],
                               t1[(fc % 2) * 2 + h][:], OP.mult, OP.mult))
                ctx.dma("A", f"s1{fc % 2}", [hbund_r[fc % 2]], [HD(fc)],
                        lambda a, fc=fc: a.dma_start(
                            out=h_t[:, fc, :], in_=hbund[fc % 2][:]))

            ctx.op("V", "lnr", [], [ACCS(0), ACCS(1)], lambda v: v.memzero(accs[:]))
            ctx.op("V", "lnr", [], [ACCQ(0), ACCQ(1)], lambda v: v.memzero(accq[:]))
            nbund = (FC + NDB - 1) // NDB

            def p4_norm(sw, dmcs):
                for dmc in dmcs:
                    dmg = sw * 4 + dmc
                    ctx.dma("S", f"s{2 + dmg % 2}", [RES1(dmg)], [r1rd2_r[dmg % 2]],
                            lambda s, dmg=dmg: s.dma_start(
                                out=r1rd2[dmg % 2], in_=res1_t[:, dmg, :]))
                    for h in range(2):
                        slot = dmc * 2 + h
                        ctx.op("V", "vq", [r1rd2_r[dmg % 2], BCS], [vtn_r[slot]],
                               lambda v, dmg=dmg, h=h, slot=slot: v.tensor_mul(
                                   vtn[slot][:],
                                   r1rd2[dmg % 2][:, h * NB:(h + 1) * NB],
                                   bcs[:, h * NB:(h + 1) * NB]))
                        ctx.op("V", "vq", [vtn_r[slot], BCMS], [vtn_r[slot]],
                               lambda v, slot=slot, h=h: v.tensor_sub(
                                   vtn[slot][:], vtn[slot][:],
                                   bcms[:, h * NB:(h + 1) * NB]))

            for sw in range(NSW):
                p4_norm(sw, (0, 1))
                for bi in range(nbund):
                    if bi == 2:
                        p4_norm(sw, (2, 3))
                    f0 = bi * NDB
                    nf = min(NDB, FC - f0)
                    bs = bi % NBS
                    ctx.dma("G", f"w{bs}", [], [dwbB_r[bs]],
                            lambda g, sw=sw, f0=f0, nf=nf, bs=bs: g.dma_start(
                                out=dwbB[bs][:, 0:nf, :],
                                in_=dwt_t[:, f0:f0 + nf, sw * 512:(sw + 1) * 512]))
                    ctx.dma("S", f"h{bs}",
                            [HD2(f0, f0 + nf)], [hrdB_r[bs]],
                            lambda s, sw=sw, f0=f0, nf=nf, bs=bs: s.dma_start(
                                out=hrdB[bs][:, 0:nf, :],
                                in_=h_t[:, f0:f0 + nf, :]))
                    for fi in range(nf):
                        fc = f0 + fi
                        for dmc in range(4):
                            for h in range(2):
                                slot = dmc * 2 + h
                                rd = [dwbB_r[bs], hrdB_r[bs]]
                                ch = (f"mm{slot}" if fc == FC - 1
                                      else f"t4{bs}")
                                wr = [PS(slot)] if fc in (0, FC - 1) else []
                                ctx.op("T", ch, rd, wr,
                                       lambda t, fi=fi, bs=bs, dmc=dmc, h=h,
                                       slot=slot, fc=fc:
                                       t.matmul(psa[slot][:, 0:NB],
                                                dwbB[bs][:, fi,
                                                         dmc * 128:(dmc + 1) * 128],
                                                hrdB[bs][:, fi,
                                                         h * NB:(h + 1) * NB],
                                                start=(fc == 0),
                                                stop=(fc == FC - 1)))
                for pair in range(2):
                    for dmc in (2 * pair, 2 * pair + 1):
                        dmg = sw * 4 + dmc
                        for h in range(2):
                            slot = dmc * 2 + h
                            r2sl = r2bund[dmg % 2][:, h * NB:(h + 1) * NB]
                            r2res = ("af", r2bund_r[dmg % 2][1] + h * NB,
                                     r2bund_r[dmg % 2][1] + (h + 1) * NB)
                            ctx.op("V", f"ev{slot}",
                                   [PS(slot), vtn_r[slot], CONST], [r2res],
                                   lambda v, slot=slot, h=h, dmg=dmg, r2sl=r2sl:
                                   v.scalar_tensor_tensor(
                                       r2sl, psa[slot][:, 0:NB],
                                       dsa[:, dmg:dmg + 1],
                                       vtn[slot][:], OP.mult, OP.add))
                    for dmc in (2 * pair, 2 * pair + 1):
                        dmg = sw * 4 + dmc
                        ctx.dma("A", f"s1{dmg % 2}", [r2bund_r[dmg % 2]],
                                [RES2(dmg)],
                                lambda a, dmg=dmg: a.dma_start(
                                    out=res2_t[:, dmg, :],
                                    in_=r2bund[dmg % 2][:]))
                    for dmc in (2 * pair, 2 * pair + 1):
                        dmg = sw * 4 + dmc
                        for h in range(2):
                            slot = dmc * 2 + h
                            r2sl = r2bund[dmg % 2][:, h * NB:(h + 1) * NB]
                            r2res = ("af", r2bund_r[dmg % 2][1] + h * NB,
                                     r2bund_r[dmg % 2][1] + (h + 1) * NB)
                            ctx.op("V", "lnr", [r2res, ACCS(h)], [ACCS(h)],
                                   lambda v, h=h, r2sl=r2sl: v.tensor_add(
                                       accs[:, h * NB:(h + 1) * NB],
                                       accs[:, h * NB:(h + 1) * NB], r2sl))
                            ctx.op("V", "vq", [r2res], [vtn_r[slot]],
                                   lambda v, slot=slot, r2sl=r2sl: v.tensor_mul(
                                       vtn[slot][:], r2sl, r2sl))
                            ctx.op("V", "lnr", [vtn_r[slot], ACCQ(h)], [ACCQ(h)],
                                   lambda v, h=h, slot=slot: v.tensor_add(
                                       accq[:, h * NB:(h + 1) * NB],
                                       accq[:, h * NB:(h + 1) * NB],
                                       vtn[slot][:]))
            ln_finalize(2)

            for tc in range(DC):
                ctx.dma("S", f"s{tc % 2}", [RES2(tc)], [r2rd_r[tc % 2]],
                        lambda s, tc=tc: s.dma_start(
                            out=r2rd[tc % 2], in_=res2_t[:, tc, :]))
                ctx.op("V", "vq", [r2rd_r[tc % 2], BCS], [otmp_r],
                       lambda v, tc=tc: v.tensor_mul(otmp[:], r2rd[tc % 2][:], bcs[:]))
                ctx.op("V", f"vr{tc % 2}", [otmp_r, BCMS], [outb_r[tc % 2]],
                       lambda v, tc=tc: v.tensor_sub(outb[tc % 2][:], otmp[:], bcms[:]))
                ctx.dma("A", f"s1{tc % 2}", [outb_r[tc % 2]], [("out", tc, tc + 1)],
                        lambda a, tc=tc: a.dma_start(
                            out=outfm[tc * 128:(tc + 1) * 128, :],
                            in_=outb[tc % 2][:]))
            ctx.mark("end")
            ctx.wait_mark("A", "end", ["ds10", "ds11"])

        pre = Ctx("pre")
        program(pre)
        sems = {n: stack.enter_context(nc.semaphore(n))
                for n in sorted(pre.channels | {f"bar{i}" for i in range(1, pre.nbar + 1)})}
        with nc.Block() as block:
            @block.gpsimd
            def _(g):
                program(Ctx("G", g, sems))

            @block.sync
            def _(s):
                program(Ctx("S", s, sems))

            @block.tensor
            def _(t):
                program(Ctx("T", t, sems))

            @block.scalar
            def _(a):
                program(Ctx("A", a, sems))

            @block.vector
            def _(v):
                program(Ctx("V", v, sems))

    return nc


def _prep_inputs(x, w1, w2, b2, mask, gw, gs, uw, us, dw, ds, D, F, TOK, n_cores=8):
    DC, FC = D // 128, F // 128
    rows = np.asarray(x).reshape(-1, D)
    f32 = np.float32
    shared = {
        "w1": np.ascontiguousarray(np.asarray(w1, f32)),
        "w2t": np.ascontiguousarray(np.asarray(w2, f32).T),
        "maskd": np.ascontiguousarray(np.asarray(mask, f32)),
        "gwt": np.ascontiguousarray(np.asarray(gw, np.int32).T),
        "uwt": np.ascontiguousarray(np.asarray(uw, np.int32).T),
        "dwt": np.ascontiguousarray(np.asarray(dw, np.int32).T),
        "gsd": np.ascontiguousarray(np.asarray(gs, f32).reshape(FC, 128).T),
        "usd": np.ascontiguousarray(np.asarray(us, f32).reshape(FC, 128).T),
        "dsd": np.ascontiguousarray(np.asarray(ds, f32).reshape(DC, 128).T),
        "b2d": np.ascontiguousarray(np.asarray(b2, f32).reshape(DC, 128).T),
    }
    in_maps = []
    for i in range(n_cores):
        m = dict(shared)
        m["xfm"] = np.ascontiguousarray(
            rows[i * TOK:(i + 1) * TOK].astype(f32, copy=False).T)
        in_maps.append(m)
    return in_maps


def kernel(L0_attn_x, L0_attn_w1, L0_attn_w2, L0_attn_b2, L0_attn_sparse_mask,
           L0_ffn_gate_w, L0_ffn_gate_s, L0_ffn_up_w, L0_ffn_up_s,
           L0_ffn_down_w, L0_ffn_down_s):
    D, F = 4096, 11008
    B, Q = L0_attn_x.shape[0], L0_attn_x.shape[1]
    n_cores = 8
    TOK = (B * Q) // n_cores
    nc = build_nc(D, F, TOK, SPW=1024)
    in_maps = _prep_inputs(L0_attn_x, L0_attn_w1, L0_attn_w2, L0_attn_b2,
                           L0_attn_sparse_mask, L0_ffn_gate_w, L0_ffn_gate_s,
                           L0_ffn_up_w, L0_ffn_up_s, L0_ffn_down_w,
                           L0_ffn_down_s, D, F, TOK, n_cores)
    res = run_bass_kernel_spmd(nc, in_maps, core_ids=list(range(n_cores)))
    out = np.empty((B * Q, D), np.float32)
    for i in range(n_cores):
        out[i * TOK:(i + 1) * TOK] = res.results[i]["outfm"].T
    return out.reshape(B, Q, D)


# version 16
# speedup vs baseline: 1.1860x; 1.0122x over previous
import numpy as np
from collections import defaultdict

import concourse.bass as bass
import concourse.mybir as mybir
from concourse.bass_utils import run_bass_kernel_spmd

F32 = mybir.dt.float32
BF16 = mybir.dt.bfloat16
I32 = mybir.dt.int32
I8 = mybir.dt.int8
AF = mybir.ActivationFunctionType
OP = mybir.AluOpType

DT_SIZE = {F32: 4, BF16: 2, I32: 4, I8: 1}


class Tracker:

    def __init__(self):
        self.spaces = defaultdict(list)

    @staticmethod
    def _merge(d, ch, val):
        if val > d.get(ch, -1):
            d[ch] = val

    def read(self, res, key):
        space, lo, hi = res
        waits = {}
        for ent in self.spaces[space]:
            if ent[0] < hi and lo < ent[1]:
                for c, v in ent[2].items():
                    self._merge(waits, c, v)
                if key is not None:
                    self._merge(ent[3], key[0], key[1])
        return waits

    def write(self, res, key):
        space, lo, hi = res
        waits = {}
        keep = []
        for ent in self.spaces[space]:
            if ent[0] < hi and lo < ent[1]:
                for c, v in ent[2].items():
                    self._merge(waits, c, v)
                for c, v in ent[3].items():
                    self._merge(waits, c, v)
                if ent[0] < lo:
                    keep.append([ent[0], lo, ent[2], ent[3]])
                if ent[1] > hi:
                    keep.append([hi, ent[1], dict(ent[2]), dict(ent[3])])
                continue
            keep.append(ent)
        keep.append([lo, hi, {key[0]: key[1]} if key else {}, {}])
        self.spaces[space] = keep
        return waits


class Ctx:
    def __init__(self, mode, eng=None, sems=None):
        self.mode = mode
        self.eng = eng
        self.sems = sems
        self.cnt = defaultdict(int)
        self.channels = set()
        self.trk = Tracker()
        self.credit = {m: defaultdict(int) for m in ("G", "S", "T", "A", "V")}
        self.nbar = 0
        self.marks = {}

    def _upd(self, ch, n=1):
        self.channels.add(ch)
        self.cnt[ch] += n
        return self.cnt[ch]

    def _collect(self, who, reads, writes, key):
        waits = {}
        for r in reads or ():
            for c, v in self.trk.read(r, key).items():
                Tracker._merge(waits, c, v)
        for w in writes or ():
            for c, v in self.trk.write(w, key).items():
                Tracker._merge(waits, c, v)
        if key is not None and waits.get(key[0], -1) >= key[1]:
            waits[key[0]] = key[1] - 1
        for c in list(waits):
            if c[0] == "d" and self.cnt[c] > waits[c] and (
                    key is None or c != key[0]):
                waits[c] = self.cnt[c]
        cr = self.credit[who]
        out = [(c, v) for c, v in waits.items() if cr[c] < v]
        for c, v in out:
            cr[c] = v
        return out

    def _emit_waits(self, pairs):
        for c, v in pairs:
            self.eng.wait_ge(self.sems[c], v)

    def dma(self, queue, stream, reads, writes, fn, extra=()):
        ch = "d" + stream
        val = self._upd(ch, 16)
        key = (ch, val)
        waits = self._collect(queue, reads, writes, key)
        for c, v in extra:
            if self.credit[queue][c] < v:
                self.credit[queue][c] = v
                waits.append((c, v))
        if self.mode == "pre" or self.mode != queue:
            return key
        self._emit_waits(waits)
        fn(self.eng).then_inc(self.sems[ch], 16)
        return key

    def op(self, eng, ch, reads, writes, fn, extra=(), as_key=None):
        key = None
        if ch is not None:
            key = (ch, self._upd(ch, 1))
        waits = self._collect(eng, reads, writes, as_key if as_key else key)
        for c, v in extra:
            if self.credit[eng][c] < v:
                self.credit[eng][c] = v
                waits.append((c, v))
        if self.mode == "pre" or self.mode != eng:
            return key
        self._emit_waits(waits)
        ins = fn(self.eng)
        if ch is not None:
            ins.then_inc(self.sems[ch], 1)
        return key

    def barrier(self):
        self.nbar += 1
        ch = f"bar{self.nbar}"
        self.channels.add(ch)
        if self.mode == "pre":
            return
        self.eng.nop().then_inc(self.sems[ch], 1)
        self.eng.wait_ge(self.sems[ch], 5)

    def mark(self, name):
        self.marks[name] = {c: self.cnt[c] for c in self.cnt if c.startswith("d")}

    def wait_mark(self, who, name, chans):
        pairs = []
        for c in chans:
            v = self.marks[name].get(c, 0)
            if v and self.credit[who][c] < v:
                self.credit[who][c] = v
                pairs.append((c, v))
        if self.mode == who:
            self._emit_waits(pairs)


def carve1f(arf_ap, off, n):
    return arf_ap[0:1, off:off + n]


def build_nc(D, F, TOK, SPW=1024, debug=False):
    DC = D // 128
    FC = F // 128
    NB = TOK // 2
    assert NB <= 512 and TOK % 2 == 0
    SPW = min(SPW, D)
    NSP = D // SPW
    SCP = SPW // 128
    TPW = min(512, D)
    NTP = D // TPW
    MB = min(4, SCP)
    NBUN = SCP // MB
    assert SCP % MB == 0 and DC % 4 == 0 and FC % 2 == 0 and MB == 4
    NSW = DC // 4

    nc = bass.Bass("TRN2", target_bir_lowering=False)

    xfm = nc.dram_tensor("xfm", [D, TOK], F32, kind="ExternalInput").ap()
    w1 = nc.dram_tensor("w1", [D, D], F32, kind="ExternalInput").ap()
    w2t = nc.dram_tensor("w2t", [D, D], F32, kind="ExternalInput").ap()
    maskd = nc.dram_tensor("maskd", [D, D], F32, kind="ExternalInput").ap()
    gwt = nc.dram_tensor("gwt", [D, F], I32, kind="ExternalInput").ap()
    uwt = nc.dram_tensor("uwt", [D, F], I32, kind="ExternalInput").ap()
    dwt = nc.dram_tensor("dwt", [F, D], I32, kind="ExternalInput").ap()
    gsd = nc.dram_tensor("gsd", [128, FC], F32, kind="ExternalInput").ap()
    usd = nc.dram_tensor("usd", [128, FC], F32, kind="ExternalInput").ap()
    dsd = nc.dram_tensor("dsd", [128, DC], F32, kind="ExternalInput").ap()
    b2d = nc.dram_tensor("b2d", [128, DC], F32, kind="ExternalInput").ap()
    outfm = nc.dram_tensor("outfm", [D, TOK], F32, kind="ExternalOutput").ap()

    ikind = "ExternalOutput" if debug else "Internal"
    attn_d = nc.dram_tensor("attn_d", [D, D], BF16, kind=ikind).ap()
    res1_d = nc.dram_tensor("res1_d", [D, TOK], F32, kind=ikind).ap()
    h_d = nc.dram_tensor("h_d", [F, TOK], BF16, kind=ikind).ap()
    res2_d = nc.dram_tensor("res2_d", [D, TOK], F32, kind=ikind).ap()

    xfm_t = xfm.rearrange("(c p) w -> p c w", p=128)
    w1_t = w1.rearrange("(c p) w -> p c w", p=128)
    w2t_t = w2t.rearrange("(c p) w -> p c w", p=128)
    mask_t = maskd.rearrange("(c p) w -> p c w", p=128)
    gwt_t = gwt.rearrange("(c p) w -> p c w", p=128)
    uwt_t = uwt.rearrange("(c p) w -> p c w", p=128)
    dwt_t = dwt.rearrange("(c p) w -> p c w", p=128)
    attn_t = attn_d.rearrange("(c p) w -> p c w", p=128)
    res1_t = res1_d.rearrange("(c p) w -> p c w", p=128)
    h_t = h_d.rearrange("(c p) w -> p c w", p=128)
    res2_t = res2_d.rearrange("(c p) w -> p c w", p=128)

    ARB_EL = 36864
    ARF_EL = 13312
    import contextlib
    stack = contextlib.ExitStack()
    with stack:
        big0 = stack.enter_context(
            nc.sbuf_tensor("big0", [128, DC * max(SPW, TOK)], BF16))
        arb = stack.enter_context(nc.sbuf_tensor("arb", [128, ARB_EL], BF16))
        arf = stack.enter_context(nc.sbuf_tensor("arf", [128, ARF_EL], F32))
        acc_s = stack.enter_context(nc.sbuf_tensor("acc_s", [128, TOK], F32))
        acc_q = stack.enter_context(nc.sbuf_tensor("acc_q", [128, TOK], F32))
        bc_s = stack.enter_context(nc.sbuf_tensor("bc_s", [128, TOK], F32))
        bc_ms = stack.enter_context(nc.sbuf_tensor("bc_ms", [128, TOK], F32))
        b2t = stack.enter_context(nc.sbuf_tensor("b2t", [128, DC], F32))
        gst = stack.enter_context(nc.sbuf_tensor("gst", [128, FC], F32))
        ust = stack.enter_context(nc.sbuf_tensor("ust", [128, FC], F32))
        dst = stack.enter_context(nc.sbuf_tensor("dst", [128, DC], F32))
        onc = stack.enter_context(nc.sbuf_tensor("onc", [128, 1], F32))
        onr = stack.enter_context(nc.sbuf_tensor("onr", [1, 128], F32))
        ps = [stack.enter_context(nc.psum_tensor(f"ps{i}", [128, 512], F32))
              for i in range(8)]

        arb_ap, arf_ap = arb.ap(), arf.ap()
        big0v = big0.ap().rearrange("p (c w) -> p c w", c=DC)
        w1p = big0v[:, :, 0:SPW]
        xbf = big0v[:, :, 0:TOK]
        r1n = big0v[:, :, 0:TOK]
        psa = [p.ap() for p in ps]

        def cb(off, *shape):
            n = int(np.prod(shape))
            v = arb_ap[:, off:off + n]
            if len(shape) == 2:
                v = v.rearrange("p (a b) -> p a b", a=shape[0])
            return v

        def cf(off, *shape):
            n = int(np.prod(shape))
            v = arf_ap[:, off:off + n]
            if len(shape) == 2:
                v = v.rearrange("p (a b) -> p a b", a=shape[0])
            return v

        def sb(off, n):
            return ("ab", off, off + n)

        def sf(off, n):
            return ("af", off, off + n)

        TPE = DC * TPW
        w2tb = [cb(0, DC, TPW), cb(TPE, DC, TPW)]
        w2tb_r = [sb(0, TPE), sb(TPE, TPE)]
        attnb = [[cb(2 * TPE + (2 * j + pq) * TPW, TPW) for pq in range(2)]
                 for j in range(4)]
        attnb_r = [[sb(2 * TPE + (2 * j + pq) * TPW, TPW) for pq in range(2)]
                   for j in range(4)]
        assert 2 * TPE + 8 * TPW <= ARB_EL
        attnL = [cb(k * DC * 256, DC, 256) for k in range(3)]
        attnL_r = [sb(k * DC * 256, DC * 256) for k in range(3)]
        gwb = [cb(0, DC, 128), cb(DC * 128, DC, 128)]
        gwb_r = [sb(0, DC * 128), sb(DC * 128, DC * 128)]
        uwb = [cb(2 * DC * 128, DC, 128), cb(3 * DC * 128, DC, 128)]
        uwb_r = [sb(2 * DC * 128, DC * 128), sb(3 * DC * 128, DC * 128)]
        hbund = [cb(4 * DC * 128, TOK), cb(4 * DC * 128 + TOK, TOK)]
        hbund_r = [sb(4 * DC * 128, TOK), sb(4 * DC * 128 + TOK, TOK)]
        NDB = 3
        NBS = 3
        dwbB = [cb(k * NDB * 512, NDB, 512) for k in range(NBS)]
        dwbB_r = [sb(k * NDB * 512, NDB * 512) for k in range(NBS)]
        hb0 = NBS * NDB * 512
        hrdB = [cb(hb0 + k * NDB * TOK, NDB, TOK) for k in range(NBS)]
        hrdB_r = [sb(hb0 + k * NDB * TOK, NDB * TOK) for k in range(NBS)]
        assert hb0 + NBS * NDB * TOK <= ARB_EL

        maskb = [cf(k * MB * TPW, MB, TPW) for k in range(3)]
        maskb_r = [sf(k * MB * TPW, MB * TPW) for k in range(3)]
        x32b = [cf(0, TOK), cf(TOK, TOK)]
        x32b_r = [sf(0, TOK), sf(TOK, TOK)]
        yact = [cf(2 * TOK + k * NB, NB) for k in range(4)]
        yact_r = [sf(2 * TOK + k * NB, NB) for k in range(4)]
        sqtmp = cf(4 * TOK, NB)
        sqtmp_r = sf(4 * TOK, NB)
        rb0 = 4 * TOK + NB
        r1bund = [cf(rb0, TOK), cf(rb0 + TOK, TOK)]
        r1bund_r = [sf(rb0, TOK), sf(rb0 + TOK, TOK)]
        t1 = [cf(k * NB, NB) for k in range(4)]
        t1_r = [sf(k * NB, NB) for k in range(4)]
        r1rd = [cf(2 * TOK, TOK), cf(3 * TOK, TOK)]
        r1rd_r = [sf(2 * TOK, TOK), sf(3 * TOK, TOK)]
        tmp32 = cf(4 * TOK, TOK)
        tmp32_r = sf(4 * TOK, TOK)
        assert 5 * TOK <= rb0 + 0 * TOK or True
        r1rd2 = [cf(0, TOK), cf(TOK, TOK)]
        r1rd2_r = [sf(0, TOK), sf(TOK, TOK)]
        vtn = [cf(2 * TOK + k * NB, NB) for k in range(8)]
        vtn_r = [sf(2 * TOK + k * NB, NB) for k in range(8)]
        r2bund = [cf(6 * TOK, TOK), cf(7 * TOK, TOK)]
        r2bund_r = [sf(6 * TOK, TOK), sf(7 * TOK, TOK)]
        r1m = [cf(5 * TOK, TOK), cf(6 * TOK, TOK)]
        r1m_r = [sf(5 * TOK, TOK), sf(6 * TOK, TOK)]
        r2rd = [cf(0, TOK), cf(TOK, TOK)]
        r2rd_r = [sf(0, TOK), sf(TOK, TOK)]
        otmp = cf(2 * TOK, TOK)
        otmp_r = sf(2 * TOK, TOK)
        outb = r2rd
        outb_r = r2rd_r
        so = 8 * TOK
        st_s = carve1f(arf_ap, so, TOK)
        st_q = carve1f(arf_ap, so + TOK, TOK)
        tA = carve1f(arf_ap, so + 2 * TOK, TOK)
        tB = carve1f(arf_ap, so + 3 * TOK, TOK)
        rstd = carve1f(arf_ap, so + 4 * TOK, TOK)
        st_r = [sf(so + k * TOK, TOK) for k in range(5)]
        assert so + 5 * TOK <= ARF_EL

        B0 = ("b0", 0, DC)
        PS = lambda i: ("ps", i, i + 1)
        ACCS = lambda h: ("accs", h, h + 1)
        ACCQ = lambda h: ("accq", h, h + 1)
        BCS = ("bcs", 0, 1)
        BCMS = ("bcms", 0, 1)
        CONST = ("cst", 0, 1)
        ONES = ("ones", 0, 1)
        ATTN = ("attn", 0, 1)
        RES1 = lambda tc: ("res1", tc, tc + 1)
        HD = lambda fc: ("hd", fc, fc + 1)
        HD2 = lambda a, b: ("hd", a, b)
        RES2 = lambda tc: ("res2", tc, tc + 1)

        onc_ap, onr_ap = onc.ap(), onr.ap()
        accs, accq = acc_s.ap(), acc_q.ap()
        bcs, bcms = bc_s.ap(), bc_ms.ap()
        b2a, gsa, usa, dsa = b2t.ap(), gst.ap(), ust.ap(), dst.ap()

        def program(ctx):
            def pgroup(slot, ch, reads, mms, extra=(), mm_reads=None):
                n = len(mms)
                fk = (ch, ctx.cnt[ch] + 1)
                for i, m in enumerate(mms):
                    rd = list(reads) if i == 0 else []
                    if mm_reads and i in mm_reads:
                        rd += mm_reads[i]
                    if n == 1:
                        ctx.op("T", ch, rd, [PS(slot)], m, extra=extra)
                    elif i == 0:
                        ctx.op("T", None, rd, [PS(slot)], m, extra=extra,
                               as_key=fk)
                    elif i == n - 1:
                        ctx.op("T", ch, rd, [], m, as_key=fk)
                    else:
                        ctx.op("T", None, rd, [], m, as_key=fk)
                return fk

            def ln_finalize(ln):
                for h in (0, 1):
                    pgroup(h, f"mm{h}", [ACCS(h), ONES], [
                        lambda t, h=h: t.matmul(psa[h][0:1, 0:NB], onc_ap[:],
                                                accs[:, h * NB:(h + 1) * NB],
                                                start=True, stop=True)])
                    pgroup(2 + h, f"mm{2 + h}", [ACCQ(h), ONES], [
                        lambda t, h=h: t.matmul(psa[2 + h][0:1, 0:NB], onc_ap[:],
                                                accq[:, h * NB:(h + 1) * NB],
                                                start=True, stop=True)])
                for i, h in ((0, 0), (1, 1), (2, 0), (3, 1)):
                    dst_t, dr = (st_s, st_r[0]) if i < 2 else (st_q, st_r[1])
                    ctx.op("V", "lc", [PS(i)], [dr],
                           lambda v, i=i, h=h, dst_t=dst_t: v.tensor_copy(
                               dst_t[:, h * NB:(h + 1) * NB], psa[i][0:1, 0:NB]))
                inv = 1.0 / D
                ctx.op("V", "lnV", [st_r[0]], [st_r[0]],
                       lambda v: v.tensor_scalar_mul(st_s[:], st_s[:], inv))
                ctx.op("V", "lnV", [st_r[1]], [st_r[1]],
                       lambda v: v.tensor_scalar_mul(st_q[:], st_q[:], inv))
                ctx.op("V", "lnV", [st_r[0]], [st_r[2]],
                       lambda v: v.tensor_mul(tA[:], st_s[:], st_s[:]))
                ctx.op("V", "lnV", [st_r[1], st_r[2]], [st_r[3]],
                       lambda v: v.tensor_sub(tB[:], st_q[:], tA[:]))
                ctx.op("V", "lnV", [st_r[3]], [st_r[3]],
                       lambda v: v.tensor_scalar_add(tB[:], tB[:], 1e-5))
                ctx.op("V", "lnV", [st_r[3]], [st_r[2]],
                       lambda v: v.reciprocal(tA[:], tB[:]))
                ctx.op("A", "lnA", [st_r[2]], [st_r[4]],
                       lambda a: a.activation(rstd[:], tA[:], AF.Sqrt))
                ctx.op("V", "lnV", [st_r[4]], [st_r[2]],
                       lambda v: v.tensor_mul(tA[:], rstd[:], rstd[:]))
                ctx.op("V", "lnV", [st_r[2], st_r[3]], [st_r[2]],
                       lambda v: v.tensor_mul(tA[:], tA[:], tB[:]))
                ctx.op("V", "lnV", [st_r[2]], [st_r[2]],
                       lambda v: v.tensor_scalar(tA[:], tA[:], -0.5, 1.5,
                                                 OP.mult, OP.add))
                ctx.op("V", "lnV", [st_r[2], st_r[4]], [st_r[4]],
                       lambda v: v.tensor_mul(rstd[:], rstd[:], tA[:]))
                ctx.op("V", "lnV", [st_r[0], st_r[4]], [st_r[2]],
                       lambda v: v.tensor_mul(tA[:], st_s[:], rstd[:]))
                for h in (0, 1):
                    pgroup(4 + h, f"mm{4 + h}", [st_r[4], ONES], [
                        lambda t, h=h: t.matmul(psa[4 + h][:, 0:NB], onr_ap[:],
                                                rstd[:, h * NB:(h + 1) * NB],
                                                start=True, stop=True)])
                    ctx.op("V", "lc", [PS(4 + h)], [BCS],
                           lambda v, h=h: v.tensor_copy(
                               bcs[:, h * NB:(h + 1) * NB], psa[4 + h][:, 0:NB]))
                    pgroup(6 + h, f"mm{6 + h}", [st_r[2], ONES], [
                        lambda t, h=h: t.matmul(psa[6 + h][:, 0:NB], onr_ap[:],
                                                tA[:, h * NB:(h + 1) * NB],
                                                start=True, stop=True)])
                    ctx.op("V", "lc", [PS(6 + h)], [BCMS],
                           lambda v, h=h: v.tensor_copy(
                               bcms[:, h * NB:(h + 1) * NB], psa[6 + h][:, 0:NB]))

            ctx.dma("S", "cst", [], [CONST], lambda s: s.dma_start(out=b2a[:], in_=b2d))
            ctx.dma("S", "cst", [], [CONST], lambda s: s.dma_start(out=gsa[:], in_=gsd))
            ctx.dma("S", "cst", [], [CONST], lambda s: s.dma_start(out=usa[:], in_=usd))
            ctx.dma("S", "cst", [], [CONST], lambda s: s.dma_start(out=dsa[:], in_=dsd))
            ctx.op("V", "v0", [], [ONES], lambda v: v.memset(onc_ap[:], 1.0))
            ctx.op("V", "v0", [], [ONES], lambda v: v.memset(onr_ap[:], 1.0))
            ctx.op("V", "v0", [], [ACCS(0), ACCS(1)], lambda v: v.memzero(accs[:]))
            ctx.op("V", "v0", [], [ACCQ(0), ACCQ(1)], lambda v: v.memzero(accq[:]))

            SH = SCP // 2
            for isp in range(NSP):
                for hp in range(2):
                    s0 = isp * SPW + hp * SH * 128
                    ctx.dma("G", "gA" if hp == 0 else "gB",
                            [], [("b0s", hp, hp + 1)],
                            lambda g, s0=s0, hp=hp: g.dma_start(
                                out=w1p[:, :, hp * SH * 128:(hp + 1) * SH * 128],
                                in_=w1_t[:, :, s0:s0 + SH * 128]))
                for itp in range(NTP):
                    gi = isp * NTP + itp
                    ctx.dma("G", f"g{gi % 2}", [], [w2tb_r[gi % 2]],
                            lambda g, itp=itp, gi=gi: g.dma_start(
                                out=w2tb[gi % 2],
                                in_=w2t_t[:, :, itp * TPW:(itp + 1) * TPW]))
                    for b in range(NBUN):
                        gmb = gi * NBUN + b
                        scg0 = isp * SCP + b * MB
                        ctx.dma("S", f"mb{gmb % 3}", [], [maskb_r[gmb % 3]],
                                lambda s, scg0=scg0, itp=itp, gmb=gmb: s.dma_start(
                                    out=maskb[gmb % 3],
                                    in_=mask_t[:, scg0:scg0 + MB,
                                               itp * TPW:(itp + 1) * TPW]))
                        for j in range(MB):
                            isc = b * MB + j
                            shalf = isc // SH
                            mms = [
                                (lambda t, dc=dc, isc=isc, gi=gi, j=j:
                                 t.matmul(psa[j][:, 0:TPW],
                                          w1p[:, dc, isc * 128:(isc + 1) * 128],
                                          w2tb[gi % 2][:, dc, :],
                                          start=(dc == 0), stop=(dc == DC - 1)))
                                for dc in range(DC)]
                            pgroup(j, f"mm{j}",
                                   [("b0s", shalf, shalf + 1), w2tb_r[gi % 2]], mms)
                            ctx.op("V", f"ev{j}",
                                   [PS(j), maskb_r[gmb % 3]],
                                   [attnb_r[j][gmb % 2]],
                                   lambda v, j=j, gmb=gmb: v.tensor_tensor(
                                       attnb[j][gmb % 2][:], psa[j][:, 0:TPW],
                                       maskb[gmb % 3][:, j, :], OP.mult))
                            ctx.dma("A", f"ab{j}{gmb % 2}",
                                    [attnb_r[j][gmb % 2]],
                                    [("attn", gmb * MB + j, gmb * MB + j + 1)],
                                    lambda a, scg0=scg0, itp=itp, j=j, gmb=gmb:
                                    a.dma_start(
                                        out=attn_t[:, scg0 + j,
                                                   itp * TPW:(itp + 1) * TPW],
                                        in_=attnb[j][gmb % 2][:]))
            ctx.mark("p1s")


            XCH = ("gA", "gB", "gC", "gD")
            CKC = DC // 4
            for k in range(4):
                wr = [("b0", k * CKC, (k + 1) * CKC), ("b0s", 0, 2)]
                ctx.dma("G", XCH[k], [], wr,
                        lambda g, k=k: g.dma_start(
                            out=big0v[:, k * CKC:(k + 1) * CKC, 0:TOK],
                            in_=xfm_t[:, k * CKC:(k + 1) * CKC, :]))
            abch = [f"dab{j}{p}" for j in range(4) for p in range(2)]
            gk2 = 0
            for it2 in range(DC // 2):
                ctx.wait_mark("S", "p1s", abch)
                ctx.dma("S", f"aL{it2 % 3}", [("attn", 0, 10 ** 6)],
                        [attnL_r[it2 % 3]],
                        lambda s, it2=it2: s.dma_start(
                            out=attnL[it2 % 3],
                            in_=attn_t[:, :, it2 * 256:(it2 + 1) * 256]))
                for tcl in range(2):
                    tc = 2 * it2 + tcl
                    ctx.dma("S", f"s{2 + tc % 2}", [], [x32b_r[tc % 2]],
                            lambda s, tc=tc: s.dma_start(
                                out=x32b[tc % 2], in_=xfm_t[:, tc, :]))
                    for h in range(2):
                        slot = gk2 % 8
                        mms = [
                            (lambda t, sc=sc, tcl=tcl, h=h, it2=it2, slot=slot:
                             t.matmul(psa[slot][:, 0:NB],
                                      attnL[it2 % 3][:, sc, tcl * 128:(tcl + 1) * 128],
                                      xbf[:, sc, h * NB:(h + 1) * NB],
                                      start=(sc == 0), stop=(sc == DC - 1)))
                            for sc in range(DC)]
                        mmrd = {k * CKC: [("b0", k * CKC, (k + 1) * CKC)]
                                for k in range(4)}
                        pgroup(slot, f"mm{slot}", [attnL_r[it2 % 3]], mms,
                               mm_reads=mmrd)
                        ctx.op("A", f"ga{gk2 % 4}",
                               [PS(slot), CONST], [yact_r[gk2 % 4]],
                               lambda a, gk2=gk2, slot=slot, tc=tc: a.activation(
                                   yact[gk2 % 4][:], psa[slot][:, 0:NB], AF.Gelu,
                                   bias=b2a[:, tc:tc + 1]))
                        rsl = r1bund[tc % 2][:, h * NB:(h + 1) * NB]
                        rres = ("af", r1bund_r[tc % 2][1] + h * NB,
                                r1bund_r[tc % 2][1] + (h + 1) * NB)
                        ctx.op("V", f"vr{gk2 % 4}",
                               [yact_r[gk2 % 4], x32b_r[tc % 2]], [rres],
                               lambda v, gk2=gk2, tc=tc, h=h, rsl=rsl: v.tensor_add(
                                   rsl, yact[gk2 % 4][:],
                                   x32b[tc % 2][:, h * NB:(h + 1) * NB]))
                        ctx.op("V", "lnr", [rres, ACCS(h)], [ACCS(h)],
                               lambda v, h=h, rsl=rsl: v.tensor_add(
                                   accs[:, h * NB:(h + 1) * NB],
                                   accs[:, h * NB:(h + 1) * NB], rsl))
                        ctx.op("V", "vq", [rres], [sqtmp_r],
                               lambda v, rsl=rsl: v.tensor_mul(sqtmp[:], rsl, rsl))
                        ctx.op("V", "lnr", [sqtmp_r, ACCQ(h)], [ACCQ(h)],
                               lambda v, h=h: v.tensor_add(
                                   accq[:, h * NB:(h + 1) * NB],
                                   accq[:, h * NB:(h + 1) * NB], sqtmp[:]))
                        gk2 += 1
                    ctx.dma("A", f"s1{tc % 2}", [r1bund_r[tc % 2]], [RES1(tc)],
                            lambda a, tc=tc: a.dma_start(
                                out=res1_t[:, tc, :], in_=r1bund[tc % 2][:]))
            ln_finalize(1)

            for fc in range(2):
                ctx.dma("G", f"g{fc % 2}", [], [gwb_r[fc % 2]],
                        lambda g, fc=fc: g.dma_start(
                            out=gwb[fc % 2], in_=gwt_t[:, :, fc * 128:(fc + 1) * 128]))
                ctx.dma("G", f"g{2 + fc % 2}", [], [uwb_r[fc % 2]],
                        lambda g, fc=fc: g.dma_start(
                            out=uwb[fc % 2], in_=uwt_t[:, :, fc * 128:(fc + 1) * 128]))
            for tc in range(DC):
                ctx.dma("S", f"s{tc % 2}", [RES1(tc)], [r1rd_r[tc % 2]],
                        lambda s, tc=tc: s.dma_start(
                            out=r1rd[tc % 2], in_=res1_t[:, tc, :]))
                ctx.op("V", "vq", [r1rd_r[tc % 2], BCS], [tmp32_r],
                       lambda v, tc=tc: v.tensor_mul(tmp32[:], r1rd[tc % 2][:], bcs[:]))
                sl = tc % 2
                ctx.op("V", f"rs{sl}", [tmp32_r, BCMS], [r1m_r[sl]],
                       lambda v, sl=sl: v.tensor_sub(r1m[sl][:], tmp32[:], bcms[:]))
                ctx.op("A", f"rn{tc // (DC // 4)}", [r1m_r[sl]],
                       [("b0", tc, tc + 1)] + ([("b0s", 0, 2)] if tc == 0 else []),
                       lambda a, tc=tc, sl=sl: a.copy(r1n[:, tc, :], r1m[sl][:]))
            for fc in range(FC):
                if fc >= 2:
                    ctx.dma("G", f"g{fc % 2}", [], [gwb_r[fc % 2]],
                            lambda g, fc=fc: g.dma_start(
                                out=gwb[fc % 2],
                                in_=gwt_t[:, :, fc * 128:(fc + 1) * 128]))
                    ctx.dma("G", f"g{2 + fc % 2}", [], [uwb_r[fc % 2]],
                            lambda g, fc=fc: g.dma_start(
                                out=uwb[fc % 2],
                                in_=uwt_t[:, :, fc * 128:(fc + 1) * 128]))
                for h in range(2):
                    slot_g = (fc % 2) * 2 + h
                    slot_u = 4 + (fc % 2) * 2 + h
                    mmsg = [
                        (lambda t, dc=dc, fc=fc, h=h, slot=slot_g:
                         t.matmul(psa[slot][:, 0:NB], gwb[fc % 2][:, dc, :],
                                  r1n[:, dc, h * NB:(h + 1) * NB],
                                  start=(dc == 0), stop=(dc == DC - 1)))
                        for dc in range(DC)]
                    q4 = DC // 4
                    b0rd = {k * q4: [("b0", k * q4, (k + 1) * q4)]
                            for k in range(4)}
                    pgroup(slot_g, f"mm{slot_g}", [gwb_r[fc % 2]], mmsg,
                           mm_reads=b0rd)
                    mmsu = [
                        (lambda t, dc=dc, fc=fc, h=h, slot=slot_u:
                         t.matmul(psa[slot][:, 0:NB], uwb[fc % 2][:, dc, :],
                                  r1n[:, dc, h * NB:(h + 1) * NB],
                                  start=(dc == 0), stop=(dc == DC - 1)))
                        for dc in range(DC)]
                    pgroup(slot_u, f"mm{slot_u}", [uwb_r[fc % 2]], mmsu,
                           mm_reads=b0rd)
                    ctx.op("A", f"ga{(fc % 2) * 2 + h}",
                           [PS(slot_g), CONST], [t1_r[(fc % 2) * 2 + h]],
                           lambda a, fc=fc, h=h, slot=slot_g: a.activation(
                               t1[(fc % 2) * 2 + h][:], psa[slot][:, 0:NB], AF.Silu,
                               scale=gsa[:, fc:fc + 1]))
                    hres = ("ab", hbund_r[fc % 2][1] + h * NB,
                            hbund_r[fc % 2][1] + (h + 1) * NB)
                    ctx.op("V", f"vr{(fc % 2) * 2 + h}",
                           [PS(slot_u), t1_r[(fc % 2) * 2 + h], CONST], [hres],
                           lambda v, fc=fc, h=h, slot=slot_u: v.scalar_tensor_tensor(
                               hbund[fc % 2][:, h * NB:(h + 1) * NB],
                               psa[slot][:, 0:NB], usa[:, fc:fc + 1],
                               t1[(fc % 2) * 2 + h][:], OP.mult, OP.mult))
                ctx.dma("A", f"s1{fc % 2}", [hbund_r[fc % 2]], [HD(fc)],
                        lambda a, fc=fc: a.dma_start(
                            out=h_t[:, fc, :], in_=hbund[fc % 2][:]))

            ctx.op("V", "lnr", [], [ACCS(0), ACCS(1)], lambda v: v.memzero(accs[:]))
            ctx.op("V", "lnr", [], [ACCQ(0), ACCQ(1)], lambda v: v.memzero(accq[:]))
            nbund = (FC + NDB - 1) // NDB

            def p4_norm(sw, dmcs):
                for dmc in dmcs:
                    dmg = sw * 4 + dmc
                    ctx.dma("S", f"s{2 + dmg % 2}", [RES1(dmg)], [r1rd2_r[dmg % 2]],
                            lambda s, dmg=dmg: s.dma_start(
                                out=r1rd2[dmg % 2], in_=res1_t[:, dmg, :]))
                    for h in range(2):
                        slot = dmc * 2 + h
                        ctx.op("V", "vq", [r1rd2_r[dmg % 2], BCS], [vtn_r[slot]],
                               lambda v, dmg=dmg, h=h, slot=slot: v.tensor_mul(
                                   vtn[slot][:],
                                   r1rd2[dmg % 2][:, h * NB:(h + 1) * NB],
                                   bcs[:, h * NB:(h + 1) * NB]))
                        ctx.op("V", "vq", [vtn_r[slot], BCMS], [vtn_r[slot]],
                               lambda v, slot=slot, h=h: v.tensor_sub(
                                   vtn[slot][:], vtn[slot][:],
                                   bcms[:, h * NB:(h + 1) * NB]))

            for sw in range(NSW):
                p4_norm(sw, (0, 1))
                for bi in range(nbund):
                    if bi == 2:
                        p4_norm(sw, (2, 3))
                    f0 = bi * NDB
                    nf = min(NDB, FC - f0)
                    bs = bi % NBS
                    ctx.dma("G", f"w{bs}", [], [dwbB_r[bs]],
                            lambda g, sw=sw, f0=f0, nf=nf, bs=bs: g.dma_start(
                                out=dwbB[bs][:, 0:nf, :],
                                in_=dwt_t[:, f0:f0 + nf, sw * 512:(sw + 1) * 512]))
                    ctx.dma("S", f"h{bs}",
                            [HD2(f0, f0 + nf)], [hrdB_r[bs]],
                            lambda s, sw=sw, f0=f0, nf=nf, bs=bs: s.dma_start(
                                out=hrdB[bs][:, 0:nf, :],
                                in_=h_t[:, f0:f0 + nf, :]))
                    for fi in range(nf):
                        fc = f0 + fi
                        for dmc in range(4):
                            for h in range(2):
                                slot = dmc * 2 + h
                                rd = [dwbB_r[bs], hrdB_r[bs]]
                                ch = (f"mm{slot}" if fc == FC - 1
                                      else f"t4{bs}")
                                wr = [PS(slot)] if fc in (0, FC - 1) else []
                                ctx.op("T", ch, rd, wr,
                                       lambda t, fi=fi, bs=bs, dmc=dmc, h=h,
                                       slot=slot, fc=fc:
                                       t.matmul(psa[slot][:, 0:NB],
                                                dwbB[bs][:, fi,
                                                         dmc * 128:(dmc + 1) * 128],
                                                hrdB[bs][:, fi,
                                                         h * NB:(h + 1) * NB],
                                                start=(fc == 0),
                                                stop=(fc == FC - 1)))
                for pair in range(2):
                    for dmc in (2 * pair, 2 * pair + 1):
                        dmg = sw * 4 + dmc
                        for h in range(2):
                            slot = dmc * 2 + h
                            r2sl = r2bund[dmg % 2][:, h * NB:(h + 1) * NB]
                            r2res = ("af", r2bund_r[dmg % 2][1] + h * NB,
                                     r2bund_r[dmg % 2][1] + (h + 1) * NB)
                            ctx.op("V", f"ev{slot}",
                                   [PS(slot), vtn_r[slot], CONST], [r2res],
                                   lambda v, slot=slot, h=h, dmg=dmg, r2sl=r2sl:
                                   v.scalar_tensor_tensor(
                                       r2sl, psa[slot][:, 0:NB],
                                       dsa[:, dmg:dmg + 1],
                                       vtn[slot][:], OP.mult, OP.add))
                    for dmc in (2 * pair, 2 * pair + 1):
                        dmg = sw * 4 + dmc
                        ctx.dma("A", f"s1{dmg % 2}", [r2bund_r[dmg % 2]],
                                [RES2(dmg)],
                                lambda a, dmg=dmg: a.dma_start(
                                    out=res2_t[:, dmg, :],
                                    in_=r2bund[dmg % 2][:]))
                    for dmc in (2 * pair, 2 * pair + 1):
                        dmg = sw * 4 + dmc
                        for h in range(2):
                            slot = dmc * 2 + h
                            r2sl = r2bund[dmg % 2][:, h * NB:(h + 1) * NB]
                            r2res = ("af", r2bund_r[dmg % 2][1] + h * NB,
                                     r2bund_r[dmg % 2][1] + (h + 1) * NB)
                            ctx.op("V", "lnr", [r2res, ACCS(h)], [ACCS(h)],
                                   lambda v, h=h, r2sl=r2sl: v.tensor_add(
                                       accs[:, h * NB:(h + 1) * NB],
                                       accs[:, h * NB:(h + 1) * NB], r2sl))
                            ctx.op("V", "vq", [r2res], [vtn_r[slot]],
                                   lambda v, slot=slot, r2sl=r2sl: v.tensor_mul(
                                       vtn[slot][:], r2sl, r2sl))
                            ctx.op("V", "lnr", [vtn_r[slot], ACCQ(h)], [ACCQ(h)],
                                   lambda v, h=h, slot=slot: v.tensor_add(
                                       accq[:, h * NB:(h + 1) * NB],
                                       accq[:, h * NB:(h + 1) * NB],
                                       vtn[slot][:]))
            ln_finalize(2)

            for tc in range(DC):
                ctx.dma("S", f"s{tc % 2}", [RES2(tc)], [r2rd_r[tc % 2]],
                        lambda s, tc=tc: s.dma_start(
                            out=r2rd[tc % 2], in_=res2_t[:, tc, :]))
                ctx.op("V", "vq", [r2rd_r[tc % 2], BCS], [otmp_r],
                       lambda v, tc=tc: v.tensor_mul(otmp[:], r2rd[tc % 2][:], bcs[:]))
                ctx.op("V", f"vr{tc % 2}", [otmp_r, BCMS], [outb_r[tc % 2]],
                       lambda v, tc=tc: v.tensor_sub(outb[tc % 2][:], otmp[:], bcms[:]))
                ctx.dma("A", f"s1{tc % 2}", [outb_r[tc % 2]], [("out", tc, tc + 1)],
                        lambda a, tc=tc: a.dma_start(
                            out=outfm[tc * 128:(tc + 1) * 128, :],
                            in_=outb[tc % 2][:]))
            ctx.mark("end")
            ctx.wait_mark("A", "end", ["ds10", "ds11"])

        pre = Ctx("pre")
        program(pre)
        sems = {n: stack.enter_context(nc.semaphore(n))
                for n in sorted(pre.channels | {f"bar{i}" for i in range(1, pre.nbar + 1)})}
        with nc.Block() as block:
            @block.gpsimd
            def _(g):
                program(Ctx("G", g, sems))

            @block.sync
            def _(s):
                program(Ctx("S", s, sems))

            @block.tensor
            def _(t):
                program(Ctx("T", t, sems))

            @block.scalar
            def _(a):
                program(Ctx("A", a, sems))

            @block.vector
            def _(v):
                program(Ctx("V", v, sems))

    return nc


def _prep_inputs(x, w1, w2, b2, mask, gw, gs, uw, us, dw, ds, D, F, TOK, n_cores=8):
    DC, FC = D // 128, F // 128
    rows = np.asarray(x).reshape(-1, D)
    f32 = np.float32
    shared = {
        "w1": np.ascontiguousarray(np.asarray(w1, f32)),
        "w2t": np.ascontiguousarray(np.asarray(w2, f32).T),
        "maskd": np.ascontiguousarray(np.asarray(mask, f32)),
        "gwt": np.ascontiguousarray(np.asarray(gw, np.int32).T),
        "uwt": np.ascontiguousarray(np.asarray(uw, np.int32).T),
        "dwt": np.ascontiguousarray(np.asarray(dw, np.int32).T),
        "gsd": np.ascontiguousarray(np.asarray(gs, f32).reshape(FC, 128).T),
        "usd": np.ascontiguousarray(np.asarray(us, f32).reshape(FC, 128).T),
        "dsd": np.ascontiguousarray(np.asarray(ds, f32).reshape(DC, 128).T),
        "b2d": np.ascontiguousarray(np.asarray(b2, f32).reshape(DC, 128).T),
    }
    in_maps = []
    for i in range(n_cores):
        m = dict(shared)
        m["xfm"] = np.ascontiguousarray(
            rows[i * TOK:(i + 1) * TOK].astype(f32, copy=False).T)
        in_maps.append(m)
    return in_maps


def kernel(L0_attn_x, L0_attn_w1, L0_attn_w2, L0_attn_b2, L0_attn_sparse_mask,
           L0_ffn_gate_w, L0_ffn_gate_s, L0_ffn_up_w, L0_ffn_up_s,
           L0_ffn_down_w, L0_ffn_down_s):
    D, F = 4096, 11008
    B, Q = L0_attn_x.shape[0], L0_attn_x.shape[1]
    n_cores = 8
    TOK = (B * Q) // n_cores
    nc = build_nc(D, F, TOK, SPW=1024)
    in_maps = _prep_inputs(L0_attn_x, L0_attn_w1, L0_attn_w2, L0_attn_b2,
                           L0_attn_sparse_mask, L0_ffn_gate_w, L0_ffn_gate_s,
                           L0_ffn_up_w, L0_ffn_up_s, L0_ffn_down_w,
                           L0_ffn_down_s, D, F, TOK, n_cores)
    res = run_bass_kernel_spmd(nc, in_maps, core_ids=list(range(n_cores)))
    out = np.empty((B * Q, D), np.float32)
    for i in range(n_cores):
        out[i * TOK:(i + 1) * TOK] = res.results[i]["outfm"].T
    return out.reshape(B, Q, D)
